# version 10 (final)
# baseline (speedup 1.0000x reference)
import math
from contextlib import ExitStack

import numpy as np
import ml_dtypes
import concourse.bass as bass
import concourse.mybir as mybir
from concourse.bass_utils import run_bass_kernel_spmd
from bass_rust import IndirectOffsetOnAxis as bass_IOA

F32 = mybir.dt.float32
F16 = mybir.dt.float16
BF16 = mybir.dt.bfloat16
AF = mybir.ActivationFunctionType
ALU = mybir.AluOpType
AX = mybir.AxisListType

D = 2048
NH = 16
SEQ = 8192
QB = 512
HALF_JOBS = [(3, 0), (2, 1)]
HALF_PREFIX = [8192, 6144]
NE = 10
EH = 1408
NHS = 11
SQK = 1.0 / math.sqrt(128.0)
CAP = 768
U32 = mybir.dt.uint32


class Sem:
    def __init__(self, h):
        self.h = h
        self.n = 0


class Prog:

    def __init__(self):
        self.ops = []

    def op(self, eng, emit, waits=(), inc=None):
        idx = len(self.ops)
        ws = []
        for w in waits:
            if w is None:
                continue
            s, v, widx = w
            assert widx < idx, f"wait on future token at op {idx}"
            ws.append((s, v))
        incs = [] if inc is None else [inc]
        for s, n in incs:
            s.n += n
        self.ops.append((eng, ws, emit, incs))
        if incs:
            s, _ = incs[0]
            return (s, s.n, idx)
        return (None, 0, idx)

    def emit_stream(self, eng_obj, name):
        hiwater = {}
        for eng, ws, emit, incs in self.ops:
            if eng != name:
                continue
            for s, v in ws:
                if hiwater.get(id(s), -1) >= v:
                    continue
                eng_obj.wait_ge(s.h, v)
                hiwater[id(s)] = v
            if emit is None:
                continue
            ins = emit(eng_obj)
            for s, n in incs:
                ins.then_inc(s.h, n)


def W(*toks):
    return [t for t in toks if t is not None]


def build_program(taps=()):
    nc = bass.Bass("TRN2", target_bir_lowering=False)

    d_x = nc.dram_tensor("xseq", [SEQ, D], F32, kind="ExternalInput").ap()
    d_wq = nc.dram_tensor("wq", [8, 128, 16, 256], F16, kind="ExternalInput").ap()
    d_bq = nc.dram_tensor("bq", [1, D], F32, kind="ExternalInput").ap()
    d_wk = nc.dram_tensor("wk", [128, 16, 256], F16, kind="ExternalInput").ap()
    d_bk = nc.dram_tensor("bk", [1, 256], F32, kind="ExternalInput").ap()
    d_wv = nc.dram_tensor("wv_aug", [128, 16, 260], F16, kind="ExternalInput").ap()
    d_bv = nc.dram_tensor("bv_aug", [1, 260], F32, kind="ExternalInput").ap()
    d_wo = nc.dram_tensor("wo", [4, 128, 16, 512], F16, kind="ExternalInput").ap()
    d_bo = nc.dram_tensor("bo", [1, D], F32, kind="ExternalInput").ap()
    d_gw = nc.dram_tensor("gwr", [128, 16, 8], F32, kind="ExternalInput").ap()
    d_wg = nc.dram_tensor("wg", [NE, 6, 128, 16, 256], F16,
                          kind="ExternalInput").ap()
    d_wu = nc.dram_tensor("wu", [NE, 6, 128, 16, 256], F16,
                          kind="ExternalInput").ap()
    d_wd = nc.dram_tensor("wd", [NE, 4, 128, 11, 512], F16,
                          kind="ExternalInput").ap()
    d_mask = nc.dram_tensor("maskT", [2048, QB], BF16, kind="ExternalInput").ap()
    d_id = nc.dram_tensor("ident", [128, 128], F16, kind="ExternalInput").ap()
    d_id32 = nc.dram_tensor("ident32", [128, 128], F32, kind="ExternalInput").ap()
    d_tbl = nc.dram_tensor("tbl", [8 * CAP + 128, 4], F32,
                           kind="ExternalOutput" if "tbl0" in taps else "Internal").ap()
    d_rt = nc.dram_tensor("rt", [2048 + 128, D], F32,
                          kind="ExternalOutput" if "rt0" in taps else "Internal").ap()
    d_rt2 = d_rt.rearrange("r (h c) -> (r h) c", h=2)
    d_out = nc.dram_tensor("out", [2048, D], F32, kind="ExternalOutput").ap()

    tap_outs = {}
    tap_dims = {
        "xn0": ([128, 16, 1024], F32), "K0": ([128, 2, 8192], F32),
        "V0": ([128, 64, 260], F32), "Q0": ([128, 16, 1024], F32),
        "mix0": ([128, 16, 1024], F32), "acc0w": ([128, 8, 2048], F32),
        "comb0": ([128, 64], F32), "acc0": ([128, 8, 2048], F32),
        "hp00": ([128, 11, 1024], F32),
    }
    for t in taps:
        if t not in tap_dims:
            continue
        shape, dt = tap_dims[t]
        tap_outs[t] = nc.dram_tensor("tap_" + t, shape, dt,
                                     kind="ExternalOutput").ap()

    p = Prog()

    with ExitStack() as st:
        E = st.enter_context
        A = nc.alloc_sbuf_tensor_at

        BASE = 16512
        g_off = [BASE]

        def galloc(name, shape, dtype):
            per = int(np.prod(shape[1:])) * mybir.dt.size(dtype)
            t = A(name, shape, dtype, offset=g_off[0])
            g_off[0] += (per + 31) // 32 * 32
            assert g_off[0] <= BASE + 22528, (name, g_off[0])
            return t

        ones = galloc("ones", [1, 512], F16)
        identS = galloc("identS", [128, 128], F16)
        bq_row = galloc("bq_row", [1, D], F16)
        bo_row = galloc("bo_row", [1, D], F16)
        bk_row = galloc("bk_row", [1, 256], F16)
        bv_row = galloc("bv_row", [1, 260], F16)
        gw_sb = galloc("gw_sb", [128, 16, 8], F16)
        qk_col = galloc("qk_col", [128, 1], F32)
        eps_c = galloc("eps_c", [128, 1], F32)
        stats = galloc("stats", [128, 24], F32)
        mv = [galloc("mv0", [128, 2], F32), galloc("mv1", [128, 2], F32)]
        rstd0 = [galloc("rstd00", [128, 1], F32), galloc("rstd01", [128, 1], F32)]
        rstd = [galloc("rstd_0", [128, 1], F32), galloc("rstd_1", [128, 1], F32)]
        rc = [galloc("rc0", [128, 1], F32), galloc("rc1", [128, 1], F32)]
        amix_tm = [galloc("amix0", [128, 128], F16), galloc("amix1", [128, 128], F16)]
        c_negm = galloc("c_negm", [128, 1], F32)
        c_e8 = galloc("c_e8", [128, 8], F32)
        c_s8 = galloc("c_s8", [128, 1], F32)
        c_rs8 = galloc("c_rs8", [128, 1], F32)
        c_pp8 = galloc("c_pp8", [128, 8], F32)
        c_m1 = galloc("c_m1", [128, 1], F32)
        c_ismax = galloc("c_ismax", [128, 8], F32)
        c_z8 = galloc("c_z8", [128, 8], F32)
        c_m2 = galloc("c_m2", [128, 1], F32)
        c_dum = galloc("c_dum", [128, 1], F32)
        combT = galloc("combT", [128, 64], F32)
        silu_t = [galloc("silu0", [128, 512], F16), galloc("silu1", [128, 512], F16)]
        id32 = galloc("id32", [128, 128], F32)
        iota_u = galloc("iota_u", [128, 1], U32)
        iota_f = galloc("iota_f", [128, 1], F32)
        trash_col = galloc("trash_col", [128, 1], F32)
        ecap8_u = galloc("ecap8_u", [8, 1], U32)
        ecap8 = galloc("ecap8", [8, 1], F32)
        carry8 = galloc("carry8", [8, 1], F32)
        t2_all = galloc("t2_all", [128, 8, 8], F32)
        ismax_all = galloc("ismax_all", [128, 8, 8], F32)
        m2nd_all = galloc("m2nd_all", [128, 8, 8], F32)
        mcols = galloc("mcols", [128, 16], F32)
        poT = galloc("poT", [128, 8], F32)
        tmp8 = galloc("tmp8", [128, 8], F32)
        dcol = [galloc("dcol0", [128, 1], F32), galloc("dcol1", [128, 1], F32)]
        dux = [galloc("du0", [128, 1], U32), galloc("du1", [128, 1], U32),
               galloc("du2", [128, 1], U32), galloc("du3", [128, 1], U32)]
        recb = [galloc("rec%d" % i, [128, 4], F32) for i in range(4)]
        sent = galloc("sent", [128, 4], F32)
        or2f = galloc("or2f", [128, 1], F32)
        or2u = galloc("or2u", [128, 1], U32)
        onec = galloc("onec", [128, 1], F16)
        ones32 = galloc("ones32", [128, 1], F32)
        rs_row = galloc("rs_row", [1, 512], F16)
        bcast_sb = galloc("bcast_sb", [128, 512], F32)

        R1 = BASE + 22528
        R2 = R1 + 32768
        R3 = R2 + 33280
        R4 = R3 + 32768
        R5 = R4 + 32768
        R6 = R5 + 32768
        assert R6 + 22784 <= 229344

        K_sb = A("K_sb", [128, 2, 8192], F16, offset=R1)
        V_sb = A("V_sb", [128, 64, 260], F16, offset=R2)
        acc = A("acc", [128, 8, D], F32, offset=R1)
        xn = A("xn", [128, 16, 1024], F16, offset=R3)
        x_t = [A("x_t0", [128, D], F32, offset=R4),
               A("x_t1", [128, D], F32, offset=R4 + 8192)]
        xn_tm = [A("xn_tm0", [128, D], F16, offset=R4 + 16384),
                 A("xn_tm1", [128, D], F16, offset=R4 + 20480)]
        xn_sc = [A("xn_sc0", [128, 16, 128], F16, offset=R4 + 24576),
                 A("xn_sc1", [128, 16, 128], F16, offset=R4 + 28672)]
        Q_sb = A("Q_sb", [128, 16, 1024], F16, offset=R4)
        wo_t = [A("wo_t0", [128, 16, 512], F16, offset=R4),
                A("wo_t1", [128, 16, 512], F16, offset=R4 + 16384)]
        wg_t = [A("wg_t0", [128, 16, 256], F16, offset=R4),
                A("wg_t1", [128, 16, 256], F16, offset=R4 + 8192)]
        wu_t = [A("wu_t0", [128, 16, 256], F16, offset=R4 + 16384),
                A("wu_t1", [128, 16, 256], F16, offset=R4 + 24576)]
        amix = A("amix", [128, 16, 1024], F16, offset=R5)
        hp = A("hp", [128, 11, 1024], F16, offset=R5)
        wk_sb = A("wk_sb", [128, 16, 256], F16, offset=R6)
        wv_sb = A("wv_sb", [128, 16, 260], F16, offset=R6 + 8192)
        wq_t = [A("wq_t0", [128, 16, 256], F16, offset=R6),
                A("wq_t1", [128, 16, 256], F16, offset=R6 + 8192)]
        mask_sb = A("mask_sb", [128, 16, 512], BF16, offset=R6)
        expT = [A("expT%d" % i, [128, 1024], BF16, offset=R6 + 16384 + 2048 * i)
                for i in range(2)]
        esum_sb = A("esum_sb", [128, 512], F32, offset=R6 + 20480)
        xw_t = [A("xw_t0", [128, D], F32, offset=R6),
                A("xw_t1", [128, D], F32, offset=R6 + 8192)]
        wd_t = [A("wd_t0", [128, 11, 512], F16, offset=R6),
                A("wd_t1", [128, 11, 512], F16, offset=R6 + 11264)]
        msk8 = A("msk8", [8, 1024], F32, offset=R6 + 17408)
        pos8 = A("pos8", [8, 1024], F32, offset=R6 + 21504)
        f_xg = [A("f_xg0", [128, D], F32, offset=R1),
                A("f_xg1", [128, D], F32, offset=R1 + 8192)]
        f_xtm = [A("f_xtm0", [128, D], F16, offset=R1 + 16384),
                 A("f_xtm1", [128, D], F16, offset=R1 + 20480)]
        f_xng = [A("f_xng0", [128, 16, 768], F16, offset=R1 + 24576),
                 A("f_xng1", [128, 16, 768], F16, offset=R1 + 49152)]
        f_hp = A("f_hp", [128, 11, 768], F16, offset=R1 + 73728)
        f_wg = [A("f_wg0", [128, 16, 256], F16, offset=R1 + 90624),
                A("f_wg1", [128, 16, 256], F16, offset=R1 + 98816)]
        f_wu = [A("f_wu0", [128, 16, 256], F16, offset=R1 + 107008),
                A("f_wu1", [128, 16, 256], F16, offset=R1 + 115200)]
        f_wd = [A("f_wd0", [128, 11, 512], F16, offset=R1 + 123392),
                A("f_wd1", [128, 11, 512], F16, offset=R1 + 134656)]
        f_rout = A("f_rout", [128, 6, 1024], F32, offset=R1 + 145920)
        f_rec = [A("f_rec0", [128, 4], F32, offset=R1 + 170496),
                 A("f_rec1", [128, 4], F32, offset=R1 + 170528)]
        f_xru = [A("f_xru0", [128, 1], U32, offset=R1 + 170560),
                 A("f_xru1", [128, 1], U32, offset=R1 + 170592)]
        f_orf = A("f_orf", [128, 8], F32, offset=R1 + 170624)
        f_comb = A("f_comb", [128, 8], F32, offset=R1 + 170656)
        assert R1 + 170688 <= 229344

        parena = E(nc.psum_tensor("parena", [128, 3072], F32))
        p32 = [parena[:, 512 * i:512 * i + 512] for i in range(6)]
        pair = [parena[:, 1024 * j:1024 * j + 1024] for j in range(2)]
        av_bank = parena[:, 2048:2560]
        sums_v = parena[0:1, 2560:3072]
        p16 = [E(nc.psum_tensor("pt%d" % i, [128, 1024], F16)) for i in range(2)]

        s_x = Sem(E(nc.semaphore("s_x")))
        s_w = Sem(E(nc.semaphore("s_w")))
        s_pe = Sem(E(nc.semaphore("s_pe")))
        s_act = Sem(E(nc.semaphore("s_act")))
        s_dve = Sem(E(nc.semaphore("s_dve")))
        s_out = Sem(E(nc.semaphore("s_out")))

        p.op("dve", lambda e: e.memset(ones[:], 1.0))
        p.op("dve", lambda e: e.memset(onec[:], 1.0))
        p.op("dve", lambda e: e.memset(ones32[:], 1.0))
        p.op("dve", lambda e: e.memset(qk_col[:], SQK))
        tok_init = p.op("dve", lambda e: e.memset(eps_c[:], 1e-5), inc=(s_dve, 1))

        def gdma(dst, src, waits=()):
            return p.op("pool", lambda e, d=dst, s=src: e.dma_start(d, s),
                        waits=waits, inc=(s_w, 16))

        gdma(bq_row[:], d_bq[:, :])
        gdma(bo_row[:], d_bo[:, :])
        gdma(bk_row[:], d_bk[:, :])
        gdma(bv_row[:], d_bv[:, :])
        gdma(gw_sb[:], d_gw[:, :, :])
        gdma(identS[:], d_id[:, :])
        tok_id32 = gdma(id32[:], d_id32[:, :])
        tok_iota = p.op("pool", lambda e: e.iota(iota_u[:], [[0, 1]], base=0,
                                                 channel_multiplier=1), inc=(s_w, 1))
        tok_ecap = p.op("pool", lambda e: e.iota(ecap8_u[:], [[0, 1]], base=0,
                                                 channel_multiplier=CAP), inc=(s_w, 1))
        tok_iotf = p.op("dve", lambda e: e.tensor_copy(iota_f[:], iota_u[:]),
                        waits=[tok_iota], inc=(s_dve, 1))
        tok_trc = p.op("dve", lambda e: e.tensor_scalar(
            trash_col[:], iota_f[:], float(8 * CAP), None, ALU.add),
            waits=[tok_iotf], inc=(s_dve, 1))
        tok_ec8 = p.op("dve", lambda e: e.tensor_copy(ecap8[:], ecap8_u[:]),
                       waits=[tok_ecap], inc=(s_dve, 1))
        p.op("dve", lambda e: e.memset(sent[:], 0.0))
        tok_sent = p.op("dve", lambda e: e.tensor_scalar(
            sent[:, 1:2], iota_f[:], 2048.0, None, ALU.add),
            waits=[tok_iotf], inc=(s_dve, 1))
        tok_pref = None
        for j in range(8 * CAP // 128):
            tok_pref = p.op("pool", lambda e, j=j: e.dma_start(
                d_tbl[128 * j:128 * j + 128, :], sent[:]),
                waits=[tok_sent], inc=(s_w, 16))
        tok_globals = tok_pref

        free = {}
        tap_done = set()

        def tap(name, sbuf_ap, wait_tok):
            if name in tap_outs and name not in tap_done:
                tap_done.add(name)
                p.op("pool", lambda e, n=name, a=sbuf_ap: e.dma_start(
                    tap_outs[n][...], a), waits=[wait_tok], inc=(s_out, 16))

        tpsum_free = {0: [], 1: []}

        for half in range(2):
            mlist = HALF_JOBS[half]
            prefix = HALF_PREFIX[half]
            ntiles = prefix // 128
            own_blocks = [4 * mlist[0], 4 * mlist[1]]
            own_tiles = {}
            for j, b in enumerate(own_blocks):
                for tb in range(4):
                    own_tiles[4 * b + tb] = 512 * j + 128 * tb


            tok_wk = gdma(wk_sb[:], d_wk[:, :, :],
                          waits=W(free.get("xw0"), free.get("xw1")))
            tok_wv = gdma(wv_sb[:], d_wv[:, :, :])

            kpsum_free = {0: W(free.get("gate_act")), 1: W(free.get("gate_act"))}
            vpsum_free = {0: W(free.get("gate_act")), 1: W(free.get("gate_act"))}
            kv_evacs = []
            last_xn_evac = None
            own_tiles2 = own_tiles

            def kv_stageA(i):
                sl = i % 2
                tok_x = p.op("sp", lambda e, sl=sl, i=i: e.dma_start(
                    x_t[sl][:], d_x[128 * i:128 * i + 128, :]),
                    waits=W(free.get("x_t%d" % sl), free.get("wo%d" % sl)),
                    inc=(s_x, 16))
                tok_st = None
                for k in range(4):
                    tok_st = p.op("dve", lambda e, sl=sl, k=k: e.bn_stats(
                        stats[:, 6 * k:6 * k + 6],
                        x_t[sl][:, 512 * k:512 * k + 512]),
                        waits=[tok_x] if k == 0 else (),
                        inc=(s_dve, 1) if k == 3 else None)
                tok_ag = p.op("dve", lambda e, sl=sl: e.bn_aggr(mv[sl][:], stats[:]),
                              waits=W(tok_st, free.get("mv%d" % sl)),
                              inc=(s_dve, 1))
                tok_sq = p.op("act", lambda e, sl=sl: e.activation(
                    rstd0[sl][:], mv[sl][:, 1:2], AF.Sqrt, bias=eps_c[:]),
                    waits=W(tok_ag, tok_init, free.get("rstd0%d" % sl)),
                    inc=(s_act, 1))
                free["mv%d" % sl] = tok_sq
                tok_rc = p.op("dve", lambda e, sl=sl: e.reciprocal(
                    rstd[sl][:], rstd0[sl][:]), waits=[tok_sq], inc=(s_dve, 1))
                free["rstd0%d" % sl] = tok_rc
                tok_ts = p.op("dve", lambda e, sl=sl: e.tensor_scalar(
                    xn_tm[sl][:], x_t[sl][:], mv[sl][:, 0:1], rstd[sl][:],
                    ALU.subtract, ALU.mult),
                    waits=W(tok_rc, free.get("xn_tm%d" % sl), free.get("wo1")),
                    inc=(s_dve, 1))
                free["x_t%d" % sl] = tok_ts
                return tok_ts

            def kv_stageB(i, tok_ts):
                nonlocal last_xn_evac, kv_evacs
                sl = i % 2
                if i in own_tiles2:
                    off = own_tiles2[i]
                    dst_ap = lambda c0, off=off: xn[:, c0:c0 + 8, off:off + 128]
                    src_ = lambda fc, off=off: xn[:, fc, off:off + 128]
                else:
                    dst_ap = lambda c0, sl=sl: xn_sc[sl][:, c0:c0 + 8, :]
                    src_ = lambda fc, sl=sl: xn_sc[sl][:, fc, :]
                tr_toks = []
                for b16 in range(2):
                    wt = W(tok_ts, tok_globals) + tpsum_free[b16]
                    tk = None
                    for k in range(8):
                        fc = 8 * b16 + k
                        tk = p.op("pe", lambda e, sl=sl, b16=b16, k=k, fc=fc:
                                  e.transpose(p16[b16][:, 128 * k:128 * k + 128],
                                              xn_tm[sl][:, 128 * fc:128 * fc + 128],
                                              identS[:]),
                                  waits=wt if k == 0 else (),
                                  inc=(s_pe, 1) if k == 7 else None)
                    tr_toks.append(tk)
                free["xn_tm%d" % sl] = tr_toks[1]
                ev_toks = []
                for b16 in range(2):
                    wt = [tr_toks[b16]]
                    if i in own_tiles2:
                        wt = wt + W(free.get("xn_own"))
                    else:
                        wt = wt + W(free.get("wo1"))
                    ev = p.op("act", lambda e, b16=b16, dst_ap=dst_ap:
                              e.activation(
                                  dst_ap(8 * b16),
                                  p16[b16].rearrange("p (c t) -> p c t", c=8),
                                  AF.Copy),
                              waits=wt, inc=(s_act, 1))
                    tpsum_free[b16] = [ev]
                    ev_toks.append(ev)
                xn_ready = ev_toks[1]
                last_xn_evac = ev_toks[1]

                par = i % 2
                for h in range(2):
                    wt = W(ev_toks[0], tok_wk, free.get("acc_out")) + kpsum_free[par]
                    p.op("pe", lambda e, h=h, par=par: e.matmul(
                        p32[[0, 2][h] + par][:, 0:128],
                        bk_row[0:1, 128 * h:128 * h + 128], ones[0:1, 0:128],
                        start=True, stop=False), waits=wt)
                    for fc in range(8):
                        p.op("pe", lambda e, h=h, par=par, fc=fc, src_=src_:
                             e.matmul(p32[[0, 2][h] + par][:, 0:128],
                                      wk_sb[:, fc, 128 * h:128 * h + 128],
                                      src_(fc), start=False, stop=False))
                wt = W(ev_toks[0], tok_wv, free.get("acc_out")) + vpsum_free[par]
                p.op("pe", lambda e, par=par: e.matmul(
                    p32[4 + par][:, 0:260], ones[0:1, 0:128], bv_row[0:1, :],
                    start=True, stop=False), waits=wt)
                for fc in range(8):
                    p.op("pe", lambda e, par=par, fc=fc, src_=src_: e.matmul(
                        p32[4 + par][:, 0:260], src_(fc), wv_sb[:, fc, :],
                        start=False, stop=False))
                for h in range(2):
                    tk = None
                    for fc in range(8, 16):
                        tk = p.op("pe", lambda e, h=h, par=par, fc=fc, src_=src_:
                                  e.matmul(p32[[0, 2][h] + par][:, 0:128],
                                           wk_sb[:, fc, 128 * h:128 * h + 128],
                                           src_(fc), start=False, stop=(fc == 15)),
                                  waits=[ev_toks[1]] if fc == 8 else (),
                                  inc=(s_pe, 1) if fc == 15 else None)
                    tok_kev = p.op("act", lambda e, h=h, par=par, i=i: e.activation(
                        K_sb[:, h, 128 * i:128 * i + 128],
                        p32[[0, 2][h] + par][:, 0:128], AF.Copy),
                        waits=[tk], inc=(s_act, 1))
                    if h == 0:
                        kpsum_free[par] = [tok_kev]
                    else:
                        kpsum_free[par] = kpsum_free[par] + [tok_kev]
                    kv_evacs.append(tok_kev)
                tk = None
                for fc in range(8, 16):
                    tk = p.op("pe", lambda e, par=par, fc=fc, src_=src_: e.matmul(
                        p32[4 + par][:, 0:260], src_(fc), wv_sb[:, fc, :],
                        start=False, stop=(fc == 15)),
                        waits=[ev_toks[1]] if fc == 8 else (),
                        inc=(s_pe, 1) if fc == 15 else None)
                tok_vev = p.op("dve", lambda e, par=par, i=i: e.tensor_copy(
                    V_sb[:, i, :], p32[4 + par][:, 0:260]),
                    waits=[tk] + W(free.get("acc_out")), inc=(s_dve, 1))
                vpsum_free[par] = [tok_vev]
                kv_evacs.append(tok_vev)
                return tok_vev

            ts_tok = {0: kv_stageA(0)}
            for i in range(ntiles):
                if i + 1 < ntiles:
                    ts_tok[i + 1] = kv_stageA(i + 1)
                tok_vev = kv_stageB(i, ts_tok.pop(i))
            free["xn_own"] = None
            kv_last_dve = tok_vev
            kv_last_act = kv_evacs[-2]
            if half == 0:
                tap("xn0", xn[:, :, :], last_xn_evac)
                tap("K0", K_sb[:, :, :], kv_last_act)
                tap("V0", V_sb[:, :, :], kv_last_dve)

            qbank_free = {k: kpsum_free[0] + kpsum_free[1] for k in range(4)}
            qbank_free[4] = vpsum_free[0]
            qbank_free[5] = vpsum_free[1]
            q_done = None
            bk_ctr = 0
            for qc in range(8):
                sl = qc % 2
                wqw = W(free.get("wq%d" % sl))
                if qc < 2:
                    wqw = wqw + [kv_last_dve]
                tok_wq = gdma(wq_t[sl][:], d_wq[qc, :, :, :], waits=wqw)
                for hl in range(2):
                    hd = 2 * qc + hl
                    for tg in range(2):
                        bk = bk_ctr % 4
                        bk_ctr += 1
                        wt = W(tok_wq, kv_last_dve, kv_last_act) + qbank_free[bk]
                        p.op("pe", lambda e, hd=hd, bk=bk: e.matmul(
                            p32[bk][:, 0:512],
                            bq_row[0:1, 128 * hd:128 * hd + 128],
                            ones[0:1, 0:512], start=True, stop=False), waits=wt)
                        tk = None
                        for fc in range(16):
                            tk = p.op("pe", lambda e, bk=bk, fc=fc, sl=sl, hl=hl,
                                      tg=tg: e.matmul(
                                p32[bk][:, 0:512],
                                wq_t[sl][:, fc, 128 * hl:128 * hl + 128],
                                xn[:, fc, 512 * tg:512 * tg + 512],
                                start=False, stop=(fc == 15)),
                                inc=(s_pe, 1) if fc == 15 else None)
                        free["wq%d" % sl] = tk
                        tok_qe = p.op("act", lambda e, hd=hd, bk=bk, tg=tg:
                                      e.activation(
                                          Q_sb[:, hd, 512 * tg:512 * tg + 512],
                                          p32[bk][:, 0:512], AF.Copy,
                                          scale=qk_col[:]),
                                      waits=[tk, tok_init], inc=(s_act, 1))
                        qbank_free[bk] = [tok_qe]
            q_last = tok_qe
            if half == 0:
                tap("Q0", Q_sb[:, :, :], q_last)

            tok_mask = gdma(mask_sb[:],
                            d_mask.rearrange("(c p) q -> p c q", p=128),
                            waits=W(free.get("wq0"), free.get("wq1")))
            pair_free = {0: qbank_free[0] + qbank_free[1],
                         1: qbank_free[2] + qbank_free[3]}
            av_free = vpsum_free[0]
            sums_free = vpsum_free[1]
            expT_free = {k: [] for k in range(2)}
            esum_free = []
            tok_me = None
            for slot in range(2):
                m = mlist[slot]
                kcs = (2048 * (m + 1)) // 128
                qb0 = 512 * slot
                for hd in range(NH):
                    kvh = hd // 8
                    es = esum_sb
                    tok_av_last = None
                    es_prev = None
                    pend = None

                    def emit_av(tok_rdy, kp, tok_es, kvh=kvh, kcs=kcs):
                        nonlocal tok_av_last
                        eb = kp % 2
                        tk = None
                        for half_k in range(2):
                            kc = 2 * kp + half_k
                            wt2 = [tok_rdy]
                            if kc == 0:
                                wt2 = wt2 + av_free
                            tk = p.op("pe", lambda e, eb=eb, kc=kc, kvh=kvh,
                                      hk=half_k, kcs=kcs: e.matmul(
                                av_bank[:, 0:512],
                                V_sb[:, kc, 130 * kvh:130 * kvh + 128],
                                expT[eb][:, 512 * hk:512 * hk + 512],
                                start=(kc == 0), stop=(kc == kcs - 1)),
                                waits=wt2,
                                inc=(s_pe, 1) if half_k == 1 else None)
                        if kp % 2 == 0:
                            tks = None
                            for half_k in range(2):
                                wt3 = sums_free if kp == 0 and half_k == 0 else ()
                                tks = p.op("pe", lambda e, eb=eb, hk=half_k,
                                           kp=kp: e.matmul(
                                    sums_v, onec[:, 0:1],
                                    expT[eb][:, 512 * hk:512 * hk + 512],
                                    start=(kp == 0 and hk == 0), stop=False),
                                    waits=wt3,
                                    inc=(s_pe, 1) if half_k == 1 else None)
                            expT_free[eb] = [tks]
                        else:
                            expT_free[eb] = [tk, tok_es]
                        if kc == kcs - 1:
                            tok_av_last = tk

                    for kp in range(kcs // 2):
                        pb = kp % 2
                        eb = kp % 2
                        wt = W(q_last, kv_last_dve) + pair_free[pb]
                        for half_k in range(2):
                            kc = 2 * kp + half_k
                            tok_qk = p.op("pe", lambda e, kvh=kvh, hd=hd, kc=kc,
                                          pb=pb, hk=half_k, qb0=qb0: e.matmul(
                                pair[pb][:, 512 * hk:512 * hk + 512],
                                K_sb[:, kvh, 128 * kc:128 * kc + 128],
                                Q_sb[:, hd, qb0:qb0 + 512], start=True, stop=True),
                                waits=wt if half_k == 0 else (),
                                inc=(s_pe, 1) if half_k == 1 else None)
                        tok_ex = p.op("act", lambda e, pb=pb, eb=eb: e.activation(
                            expT[eb][:], pair[pb][:, 0:1024], AF.Exp),
                            waits=[tok_qk] + expT_free[eb], inc=(s_act, 1))
                        pair_free[pb] = [tok_ex]
                        tok_rdy = tok_ex
                        if 2 * kp >= kcs - 16:
                            mi = 2 * kp - (kcs - 16)
                            tok_rdy = p.op("dve", lambda e, eb=eb, mi=mi:
                                           e.tensor_tensor(
                                               expT[eb][:], expT[eb][:],
                                               mask_sb[:, mi:mi + 2, :]
                                               .rearrange("p c q -> p (c q)"),
                                               ALU.mult),
                                           waits=[tok_ex, tok_mask],
                                           inc=(s_dve, 1))
                        if kp % 2 == 1:
                            if kp == 1:
                                tok_es = p.op("dve", lambda e, eb=eb, es=es:
                                              e.tensor_tensor(
                                                  es[:], expT[eb][:, 0:512],
                                                  expT[eb][:, 512:1024],
                                                  ALU.add),
                                              waits=[tok_rdy] + esum_free,
                                              inc=(s_dve, 1))
                            else:
                                t1 = p.op("dve", lambda e, eb=eb, es=es:
                                          e.tensor_tensor(
                                              es[:], es[:], expT[eb][:, 0:512],
                                              ALU.add),
                                          waits=W(tok_rdy, es_prev),
                                          inc=(s_dve, 1))
                                tok_es = p.op("dve", lambda e, eb=eb, es=es:
                                              e.tensor_tensor(
                                                  es[:], es[:],
                                                  expT[eb][:, 512:1024],
                                                  ALU.add),
                                              waits=[t1], inc=(s_dve, 1))
                            es_prev = tok_es
                        else:
                            tok_es = None
                        if pend is not None:
                            emit_av(*pend)
                        pend = (tok_rdy, kp, tok_es)
                    emit_av(*pend)
                    tok_sums = p.op("pe", lambda e, es=es: e.matmul(
                        sums_v, ones32[:, 0:1], es[:], start=False, stop=True),
                        waits=[es_prev], inc=(s_pe, 1))
                    esum_free = [tok_sums]
                    tok_cp = p.op("act", lambda e: e.activation(
                        rs_row[:], sums_v, AF.Copy),
                        waits=W(tok_sums, free.get("rs_row")), inc=(s_act, 1))
                    sums_free = [tok_cp]
                    bcb = (kcs // 2) % 2
                    tok_bc = p.op("pe", lambda e, bcb=bcb: e.matmul(
                        pair[bcb][:, 0:512], ones[0:1, 0:128], rs_row[:],
                        start=True, stop=True),
                        waits=[tok_cp] + pair_free[bcb], inc=(s_pe, 1))
                    free["rs_row"] = tok_bc
                    tok_rc2 = p.op("dve", lambda e, bcb=bcb: e.reciprocal(
                        bcast_sb[:], pair[bcb][:, 0:512]),
                        waits=W(tok_bc, free.get("bcast_sb")), inc=(s_dve, 1))
                    pair_free[bcb] = [tok_rc2]
                    tok_me = p.op("dve", lambda e, hd=hd, qb0=qb0:
                                  e.tensor_tensor(
                                      amix[:, hd, qb0:qb0 + 512],
                                      av_bank[:, 0:512], bcast_sb[:],
                                      ALU.mult),
                                  waits=W(tok_av_last, tok_rc2), inc=(s_dve, 1))
                    free["bcast_sb"] = tok_me
                    av_free = [tok_me]
            mix_last = tok_me
            if half == 0:
                tap("mix0", amix[:, :, :], mix_last)

            wo_bank_free = {k: [] for k in range(6)}
            wo_pe_last = None
            tok_ae = None
            for dmc in range(4):
                sl = dmc % 2
                tok_wo = gdma(wo_t[sl][:], d_wo[dmc, :, :, :],
                              waits=W(free.get("wo%d" % sl), mix_last))
                for t8 in range(8):
                    bk = (dmc * 8 + t8) % 6
                    wt = W(tok_wo, mix_last) + wo_bank_free[bk]
                    p.op("pe", lambda e, dmc=dmc, bk=bk: e.matmul(
                        p32[bk][:, 0:512], ones[0:1, 0:128],
                        bo_row[0:1, 512 * dmc:512 * dmc + 512],
                        start=True, stop=False), waits=wt)
                    tk = None
                    for fc in range(16):
                        tk = p.op("pe", lambda e, bk=bk, fc=fc, sl=sl, t8=t8:
                                  e.matmul(p32[bk][:, 0:512],
                                           amix[:, fc, 128 * t8:128 * t8 + 128],
                                           wo_t[sl][:, fc, :],
                                           start=False, stop=(fc == 15)),
                                  inc=(s_pe, 1) if fc == 15 else None)
                    free["wo%d" % sl] = tk
                    wo_pe_last = tk
                    tok_ae = p.op("dve", lambda e, bk=bk, t8=t8, dmc=dmc:
                                  e.tensor_copy(
                                      acc[:, t8, 512 * dmc:512 * dmc + 512],
                                      p32[bk][:, 0:512]),
                                  waits=[tk], inc=(s_dve, 1))
                    wo_bank_free[bk] = [tok_ae]
            acc_w_last = tok_ae
            tok_ax = None
            for t8 in range(8):
                sl = t8 % 2
                j, tb = t8 // 4, t8 % 4
                row0 = 512 * own_blocks[j] + 128 * tb
                tok_xw = p.op("sp", lambda e, sl=sl, row0=row0: e.dma_start(
                    xw_t[sl][:], d_x[row0:row0 + 128, :]),
                    waits=W(free.get("xw%d" % sl), mix_last), inc=(s_x, 16))
                tok_ax = p.op("dve", lambda e, sl=sl, t8=t8: e.tensor_tensor(
                    acc[:, t8, :], acc[:, t8, :], xw_t[sl][:], ALU.add),
                    waits=[tok_xw, acc_w_last], inc=(s_dve, 1))
                free["xw%d" % sl] = tok_ax
            accx_last = tok_ax
            if half == 0:
                tap("acc0w", acc[:, :, :], accx_last)

            comb_prev = None
            gate_pe_last = None
            gate_act_last = None
            msk_ev_last = None
            for t8 in range(8):
                bk = t8 % 2
                wt = W(tok_globals, comb_prev) + wo_bank_free[bk]
                tk = None
                for fc in range(16):
                    tk = p.op("pe", lambda e, bk=bk, fc=fc, t8=t8: e.matmul(
                        p32[bk][:, 0:8], xn[:, fc, 128 * t8:128 * t8 + 128],
                        gw_sb[:, fc, :], start=(fc == 0), stop=(fc == 15)),
                        waits=wt if fc == 0 else (),
                        inc=(s_pe, 1) if fc == 15 else None)
                gate_pe_last = tk
                tok_nm = p.op("dve", lambda e, bk=bk: e.reduce_max(
                    c_negm[:], p32[bk][:, 0:8], AX.X, negate=True),
                    waits=W(tk, comb_prev), inc=(s_dve, 1))
                tok_e8 = p.op("act", lambda e, bk=bk: e.activation(
                    c_e8[:], p32[bk][:, 0:8], AF.Exp, bias=c_negm[:],
                    accum_out=c_s8[:]), waits=[tok_nm], inc=(s_act, 1))
                tok_e8b = p.op("act", lambda e: e.activation(
                    c_dum[:], c_negm[:], AF.Square), waits=[tok_e8],
                    inc=(s_act, 1))
                gate_act_last = tok_e8b
                wo_bank_free[bk] = [tok_e8b]
                tok_rs = p.op("dve", lambda e: e.reciprocal(c_rs8[:], c_s8[:]),
                              waits=[tok_e8b], inc=(s_dve, 1))
                tok_pp = p.op("dve", lambda e: e.tensor_scalar(
                    c_pp8[:], c_e8[:], c_rs8[:], None, ALU.mult),
                    waits=[tok_rs], inc=(s_dve, 1))
                tok_m1 = p.op("dve", lambda e: e.reduce_max(
                    c_m1[:], c_pp8[:], AX.X), waits=[tok_pp], inc=(s_dve, 1))
                tok_im = p.op("dve", lambda e, t8=t8: e.tensor_scalar(
                    ismax_all[:, t8, :], c_pp8[:], c_m1[:], None, ALU.is_ge),
                    waits=[tok_m1], inc=(s_dve, 1))
                tok_z8 = p.op("dve", lambda e, t8=t8: e.scalar_tensor_tensor(
                    c_z8[:], ismax_all[:, t8, :], -1e30, c_pp8[:],
                    ALU.mult, ALU.add), waits=[tok_im], inc=(s_dve, 1))
                tok_m2 = p.op("dve", lambda e: e.reduce_max(
                    c_m2[:], c_z8[:], AX.X), waits=[tok_z8], inc=(s_dve, 1))
                tok_t2 = p.op("dve", lambda e, t8=t8: e.tensor_scalar(
                    t2_all[:, t8, :], c_pp8[:], c_m2[:], None, ALU.is_ge),
                    waits=[tok_m2], inc=(s_dve, 1))
                tok_ms = p.op("dve", lambda e, t8=t8: e.tensor_sub(
                    m2nd_all[:, t8, :], t2_all[:, t8, :], ismax_all[:, t8, :]),
                    waits=[tok_t2], inc=(s_dve, 1))
                tok_c1 = p.op("dve", lambda e, t8=t8: e.tensor_copy(
                    mcols[:, 2 * t8:2 * t8 + 1], c_m1[:]), waits=[tok_ms],
                    inc=(s_dve, 1))
                tok_c2 = p.op("dve", lambda e, t8=t8: e.tensor_copy(
                    mcols[:, 2 * t8 + 1:2 * t8 + 2], c_m2[:]), waits=[tok_c1],
                    inc=(s_dve, 1))
                comb_prev = tok_c2
                tbk = 2 + t8 % 2
                tok_mtr = p.op("pe", lambda e, t8=t8, tbk=tbk: e.transpose(
                    p32[tbk][0:8, 0:128], t2_all[:, t8, :], id32[:]),
                    waits=W(tok_ms, tok_id32) + wo_bank_free.get(tbk, []),
                    inc=(s_pe, 1))
                tok_mev = p.op("dve", lambda e, t8=t8, tbk=tbk: e.tensor_copy(
                    msk8[:, 128 * t8:128 * t8 + 128], p32[tbk][0:8, 0:128]),
                    waits=[tok_mtr], inc=(s_dve, 1))
                wo_bank_free[tbk] = [tok_mev]
                msk_ev_last = tok_mev
            free["gate_act"] = gate_act_last
            free["xn_own"] = gate_pe_last

            if half == 0:
                tok_scan = p.op("dve", lambda e: e.tensor_tensor_scan(
                    pos8[:], msk8[:], msk8[:], 0.0, ALU.add, ALU.bypass),
                    waits=[msk_ev_last], inc=(s_dve, 1))
            else:
                tok_scan = p.op("dve", lambda e: e.tensor_tensor_scan(
                    pos8[:], msk8[:], msk8[:], carry8[:], ALU.add, ALU.bypass),
                    waits=[msk_ev_last], inc=(s_dve, 1))
            tok_cy = p.op("dve", lambda e: e.tensor_copy(
                carry8[:], pos8[:, 1023:1024]), waits=[tok_scan], inc=(s_dve, 1))
            tok_px = p.op("dve", lambda e: e.tensor_sub(
                pos8[:], pos8[:], msk8[:]), waits=[tok_cy], inc=(s_dve, 1))
            tok_ov = p.op("dve", lambda e: e.tensor_scalar(
                msk8[:], pos8[:], float(CAP), None, ALU.is_ge),
                waits=[tok_px], inc=(s_dve, 1))
            tok_po = p.op("dve", lambda e: e.scalar_tensor_tensor(
                pos8[:], msk8[:], float(16 * CAP), pos8[:], ALU.mult, ALU.add),
                waits=[tok_ov], inc=(s_dve, 1))
            tok_po2 = p.op("dve", lambda e: e.tensor_scalar(
                pos8[:], pos8[:], ecap8[:], None, ALU.add),
                waits=W(tok_po, tok_ec8), inc=(s_dve, 1))

            rec_sc_last = None
            for t8 in range(8):
                tbk = 2 + t8 % 2
                tok_ptr = p.op("pe", lambda e, t8=t8, tbk=tbk: e.transpose(
                    p32[tbk][:, 0:8], pos8[0:8, 128 * t8:128 * t8 + 128],
                    id32[0:8, 0:8]),
                    waits=W(tok_po2) + wo_bank_free[tbk], inc=(s_pe, 1))
                tok_pev = p.op("dve", lambda e, tbk=tbk: e.tensor_copy(
                    poT[:], p32[tbk][:, 0:8]), waits=W(
                        tok_ptr, free.get("poT")), inc=(s_dve, 1))
                wo_bank_free[tbk] = [tok_pev]
                j, tb = t8 // 4, t8 % 4
                xrow0 = float(512 * own_blocks[j] + 128 * tb)
                orow0 = float(1024 * half + 128 * t8)
                tok_leg = None
                for leg in range(2):
                    lm = ismax_all if leg == 0 else m2nd_all
                    rr = recb[2 * (t8 % 2) + leg]
                    du = dux[2 * (t8 % 2) + leg]
                    tok_mu = p.op("dve", lambda e, lm=lm, t8=t8: e.tensor_mul(
                        tmp8[:], lm[:, t8, :], poT[:]),
                        waits=W(tok_pev, free.get("tmp8")), inc=(s_dve, 1))
                    tok_d = p.op("dve", lambda e, leg=leg: e.reduce_sum(
                        dcol[leg][:], tmp8[:], AX.X), waits=[tok_mu],
                        inc=(s_dve, 1))
                    free["tmp8"] = tok_d
                    tok_dc = p.op("dve", lambda e, leg=leg: e.tensor_tensor(
                        dcol[leg][:], dcol[leg][:], trash_col[:], ALU.min),
                        waits=W(tok_d, tok_trc), inc=(s_dve, 1))
                    tok_du = p.op("dve", lambda e, leg=leg, du=du: e.tensor_copy(
                        du[:], dcol[leg][:]),
                        waits=W(tok_dc, free.get(du.name)), inc=(s_dve, 1))
                    tok_r0 = p.op("dve", lambda e, rr=rr, xrow0=xrow0:
                                  e.tensor_scalar(rr[:, 0:1], iota_f[:], xrow0,
                                                  None, ALU.add),
                                  waits=W(free.get(rr.name), tok_iotf),
                                  inc=(s_dve, 1))
                    tok_r1 = p.op("dve", lambda e, rr=rr, orow0=orow0:
                                  e.tensor_scalar(rr[:, 1:2], iota_f[:], orow0,
                                                  None, ALU.add),
                                  waits=[tok_r0], inc=(s_dve, 1))
                    tok_r2 = p.op("dve", lambda e, rr=rr, t8=t8, leg=leg:
                                  e.tensor_copy(rr[:, 2:3],
                                                mcols[:, 2 * t8 + leg:2 * t8 + leg + 1]),
                                  waits=[tok_r1], inc=(s_dve, 1))
                    tok_sc = p.op("pool", lambda e, rr=rr, du=du:
                                  e.indirect_dma_start(
                                      d_tbl[:, :], bass_IOA(du[:, 0:1], 0),
                                      rr[:], None),
                                  waits=W(tok_du, tok_r2, tok_globals),
                                  inc=(s_w, 16))
                    free[rr.name] = tok_sc
                    free[du.name] = tok_sc
                    rec_sc_last = tok_sc
                free["poT"] = tok_mu
            free["rec_scatter_h%d" % half] = rec_sc_last

            out_view = d_out[1024 * half:1024 * half + 1024, :] \
                .rearrange("(t p) d -> p t d", p=128)
            tok_out = p.op("sp", lambda e, ov=out_view: e.dma_start(ov, acc[:]),
                           waits=[accx_last], inc=(s_out, 16))
            free["acc_out"] = tok_out
            free["x_t0"] = None
            free["x_t1"] = None


        ffn_bar = (s_out, s_out.n, len(p.ops) - 1)
        for eng in ("sp", "pool", "pe", "act", "dve"):
            p.op(eng, None, waits=[ffn_bar])
        recs_done = free["rec_scatter_h1"]

        OWN_ALL = [12, 0, 8, 4]

        def xrow_of(th):
            return 512 * OWN_ALL[th // 4] + 128 * (th % 4)

        ffree = {}
        fbank = {k: [] for k in range(6)}
        ftp = {0: [], 1: []}
        scatter_prev = [None]

        jobs = [("direct", 8, 6, 0, "init"), ("direct", 8, 6, 6, "init"),
                ("direct", 8, 4, 12, "init"), ("direct", 9, 6, 0, "add"),
                ("direct", 9, 6, 6, "add"), ("direct", 9, 4, 12, "add")]
        for e_i in range(8):
            jobs.append(("gather", e_i, 6, 0, "scat"))

        for jn, (kind, e_i, ntl, tile0, mode) in enumerate(jobs):
            buf = jn % 2
            xng = f_xng[buf]

            def f_stageA(j, kind=kind, e_i=e_i, tile0=tile0):
                sl = j % 2
                if kind == "gather":
                    tok_rec = p.op("sp", lambda e, sl=sl, j=j, e_i=e_i:
                                   e.dma_start(f_rec[sl][:],
                                               d_tbl[e_i * CAP + 128 * j:
                                                     e_i * CAP + 128 * j + 128, :]),
                                   waits=W(ffree.get("rec%d" % sl), recs_done),
                                   inc=(s_x, 16))
                    tok_xr = p.op("dve", lambda e, sl=sl: e.tensor_copy(
                        f_xru[sl][:], f_rec[sl][:, 0:1]),
                        waits=W(tok_rec, ffree.get("xru%d" % sl)),
                        inc=(s_dve, 1))
                    tok_cm = p.op("dve", lambda e, sl=sl, j=j: e.tensor_copy(
                        f_comb[:, j:j + 1], f_rec[sl][:, 2:3]),
                        waits=[tok_xr], inc=(s_dve, 1))
                    tok_of = p.op("dve", lambda e, sl=sl, j=j: e.tensor_copy(
                        f_orf[:, j:j + 1], f_rec[sl][:, 1:2]),
                        waits=[tok_cm], inc=(s_dve, 1))
                    ffree["rec%d" % sl] = tok_of
                    tok_x = p.op("pool", lambda e, sl=sl: e.indirect_dma_start(
                        f_xg[sl][:], None, d_x[:, :],
                        bass_IOA(f_xru[sl][:, 0:1], 0)),
                        waits=W(tok_of, ffree.get("f_xg%d" % sl)),
                        inc=(s_w, 16))
                    ffree["xru%d" % sl] = tok_x
                else:
                    row0 = xrow_of(tile0 + j)
                    tok_x = p.op("sp", lambda e, sl=sl, row0=row0: e.dma_start(
                        f_xg[sl][:], d_x[row0:row0 + 128, :]),
                        waits=W(ffree.get("f_xg%d" % sl)), inc=(s_x, 16))
                tok_st = None
                for k in range(4):
                    tok_st = p.op("dve", lambda e, sl=sl, k=k: e.bn_stats(
                        stats[:, 6 * k:6 * k + 6],
                        f_xg[sl][:, 512 * k:512 * k + 512]),
                        waits=[tok_x] if k == 0 else (),
                        inc=(s_dve, 1) if k == 3 else None)
                tok_ag = p.op("dve", lambda e, sl=sl: e.bn_aggr(mv[sl][:], stats[:]),
                              waits=W(tok_st, free.get("mv%d" % sl)),
                              inc=(s_dve, 1))
                tok_sq = p.op("act", lambda e, sl=sl: e.activation(
                    rstd0[sl][:], mv[sl][:, 1:2], AF.Sqrt, bias=eps_c[:]),
                    waits=W(tok_ag, free.get("rstd0%d" % sl)), inc=(s_act, 1))
                free["mv%d" % sl] = tok_sq
                tok_rc = p.op("dve", lambda e, sl=sl: e.reciprocal(
                    rstd[sl][:], rstd0[sl][:]), waits=[tok_sq], inc=(s_dve, 1))
                free["rstd0%d" % sl] = tok_rc
                tok_ts = p.op("dve", lambda e, sl=sl: e.tensor_scalar(
                    f_xtm[sl][:], f_xg[sl][:], mv[sl][:, 0:1], rstd[sl][:],
                    ALU.subtract, ALU.mult),
                    waits=W(tok_rc, ffree.get("xtm%d" % sl)), inc=(s_dve, 1))
                ffree["f_xg%d" % sl] = tok_ts
                return tok_ts

            def f_stageB(j, tok_ts, buf=buf):
                sl = j % 2
                tr_toks = []
                for b16 in range(2):
                    wt = W(tok_ts, tok_globals) + ftp[b16]
                    tk = None
                    for k in range(8):
                        fc = 8 * b16 + k
                        tk = p.op("pe", lambda e, sl=sl, b16=b16, k=k, fc=fc:
                                  e.transpose(p16[b16][:, 128 * k:128 * k + 128],
                                              f_xtm[sl][:, 128 * fc:128 * fc + 128],
                                              identS[:]),
                                  waits=wt if k == 0 else (),
                                  inc=(s_pe, 1) if k == 7 else None)
                    tr_toks.append(tk)
                ffree["xtm%d" % sl] = tr_toks[1]
                evl = None
                for b16 in range(2):
                    evl = p.op("act", lambda e, b16=b16, j=j, buf=buf:
                               e.activation(
                                   f_xng[buf][:, 8 * b16:8 * b16 + 8,
                                              128 * j:128 * j + 128],
                                   p16[b16].rearrange("p (c t) -> p c t", c=8),
                                   AF.Copy),
                               waits=[tr_toks[b16]] + W(ffree.get("xng%d" % buf)),
                               inc=(s_act, 1))
                    ftp[b16] = [evl]
                return evl

            ts_prev = f_stageA(0)
            xng_ready = None
            for j in range(ntl):
                ts_next = f_stageA(j + 1) if j + 1 < ntl else None
                xng_ready = f_stageB(j, ts_prev)
                ts_prev = ts_next

            tgroups = [(0, 512), (512, 256)] if ntl == 6 else [(0, 512)]
            hcl = [(256 * c, 256 if c < 5 else 128) for c in range(6)]
            hp_free_l = ffree.get("hp", [])
            tok_hp = None
            for ci, (hoff, cw) in enumerate(hcl):
                sl = ci % 2
                tok_wg = gdma(f_wg[sl][:, :, :], d_wg[e_i, ci, :, :, :],
                              waits=W(ffree.get("fwg%d" % sl)))
                tok_wu = gdma(f_wu[sl][:, :, :], d_wu[e_i, ci, :, :, :],
                              waits=W(ffree.get("fwu%d" % sl)))
                for hs in range(cw // 128):
                    hglob = hoff // 128 + hs
                    for tg, (t0, tw) in enumerate(tgroups):
                        gb = 2 * ((hglob + tg) % 2)
                        spr = (hglob + tg) % 2
                        wt = W(tok_wg, xng_ready) + fbank[gb]
                        tkg = None
                        for fc in range(16):
                            tkg = p.op("pe", lambda e, gb=gb, fc=fc, sl=sl,
                                       hs=hs, t0=t0, tw=tw, buf=buf: e.matmul(
                                p32[gb][:, 0:tw],
                                f_wg[sl][:, fc, 128 * hs:128 * hs + 128],
                                f_xng[buf][:, fc, t0:t0 + tw],
                                start=(fc == 0), stop=(fc == 15)),
                                waits=wt if fc == 0 else (),
                                inc=(s_pe, 1) if fc == 15 else None)
                        wt = W(tok_wu) + fbank[gb + 1]
                        tku = None
                        for fc in range(16):
                            tku = p.op("pe", lambda e, gb=gb, fc=fc, sl=sl,
                                       hs=hs, t0=t0, tw=tw, buf=buf: e.matmul(
                                p32[gb + 1][:, 0:tw],
                                f_wu[sl][:, fc, 128 * hs:128 * hs + 128],
                                f_xng[buf][:, fc, t0:t0 + tw],
                                start=(fc == 0), stop=(fc == 15)),
                                waits=wt if fc == 0 else (),
                                inc=(s_pe, 1) if fc == 15 else None)
                        ffree["fwg%d" % sl] = tkg
                        ffree["fwu%d" % sl] = tku
                        tok_si = p.op("act", lambda e, gb=gb, spr=spr, tw=tw:
                                      e.activation(silu_t[spr][:, 0:tw],
                                                   p32[gb][:, 0:tw], AF.Silu),
                                      waits=W(tkg, free.get("silu%d" % spr)),
                                      inc=(s_act, 1))
                        fbank[gb] = [tok_si]
                        tok_hp = p.op("dve", lambda e, hglob=hglob, t0=t0, tw=tw,
                                      gb=gb, spr=spr: e.tensor_tensor(
                            f_hp[:, hglob, t0:t0 + tw],
                            silu_t[spr][:, 0:tw], p32[gb + 1][:, 0:tw], ALU.mult),
                            waits=W(tku, tok_si) + hp_free_l, inc=(s_dve, 1))
                        free["silu%d" % spr] = tok_hp
                        fbank[gb + 1] = [tok_hp]
            hp_ready = tok_hp
            ffree["xng%d" % buf] = tok_hp

            hp_pe_last = None
            job_sc = []
            for dmh in range(2):
                ev_toks = {}
                for q in range(2):
                    dmc = 2 * dmh + q
                    sl = dmc % 2
                    tok_wd = gdma(f_wd[sl][:], d_wd[e_i, dmc, :, :, :],
                                  waits=W(ffree.get("fwd%d" % sl)))
                    for t in range(ntl):
                        bk = 4 + (t % 2)
                        wt = W(tok_wd, hp_ready) + fbank[bk]
                        if dmh == 1 and q == 0:
                            wt = wt + job_sc[-ntl:]
                        tk = None
                        for hs in range(NHS):
                            tk = p.op("pe", lambda e, bk=bk, hs=hs, t=t, sl=sl:
                                      e.matmul(p32[bk][:, 0:512],
                                               f_hp[:, hs, 128 * t:128 * t + 128],
                                               f_wd[sl][:, hs, 0:512],
                                               start=(hs == 0),
                                               stop=(hs == NHS - 1)),
                                      waits=wt if hs == 0 else (),
                                      inc=(s_pe, 1) if hs == NHS - 1 else None)
                        hp_pe_last = tk
                        ffree["fwd%d" % sl] = tk
                        if kind == "gather":
                            tok_ev = p.op("dve", lambda e, bk=bk, t=t, q=q:
                                          e.scalar_tensor_tensor(
                                f_rout[:, t, 512 * q:512 * q + 512],
                                p32[bk][:, 0:512], f_comb[:, t:t + 1],
                                f_rout[:, t, 512 * q:512 * q + 512],
                                ALU.mult, ALU.bypass), waits=[tk],
                                inc=(s_dve, 1))
                        else:
                            tok_ev = p.op("dve", lambda e, bk=bk, t=t, q=q:
                                          e.tensor_copy(
                                f_rout[:, t, 512 * q:512 * q + 512],
                                p32[bk][:, 0:512]), waits=[tk], inc=(s_dve, 1))
                        fbank[bk] = [tok_ev]
                        ev_toks.setdefault(t, []).append(tok_ev)
                for t in range(ntl):
                    if kind == "gather":
                        t2f = p.op("dve", lambda e, t=t, dmh=dmh: e.tensor_scalar(
                            or2f[:], f_orf[:, t:t + 1], 2.0, float(dmh),
                            ALU.mult, ALU.add),
                            waits=ev_toks[t] + W(ffree.get("or2")), inc=(s_dve, 1))
                        t2u = p.op("dve", lambda e: e.tensor_copy(or2u[:], or2f[:]),
                                   waits=[t2f], inc=(s_dve, 1))
                        tok_sc = p.op("pool", lambda e, t=t: e.indirect_dma_start(
                            d_rt2[:, :], bass_IOA(or2u[:, 0:1], 0),
                            f_rout[:, t, :], None,
                            oob_is_err=True, compute_op=ALU.add),
                            waits=[t2u] + W(scatter_prev[0]), inc=(s_w, 16))
                        ffree["or2"] = tok_sc
                    else:
                        r0 = 128 * (tile0 + t)
                        if mode == "init":
                            tok_sc = p.op("pool", lambda e, t=t, r0=r0, dmh=dmh:
                                          e.dma_start(
                                d_rt[r0:r0 + 128, 1024 * dmh:1024 * dmh + 1024],
                                f_rout[:, t, :]),
                                waits=ev_toks[t] + W(scatter_prev[0]),
                                inc=(s_w, 16))
                        else:
                            tok_sc = p.op("pool", lambda e, t=t, r0=r0, dmh=dmh:
                                          e.dma_start(
                                d_rt[r0:r0 + 128, 1024 * dmh:1024 * dmh + 1024],
                                f_rout[:, t, :], accum_op=ALU.add),
                                waits=ev_toks[t] + W(scatter_prev[0]),
                                inc=(s_w, 16))
                    job_sc.append(tok_sc)
            scatter_prev[0] = job_sc[-1]
            ffree["hp"] = [hp_pe_last]

        for t in range(16):
            sl = t % 2
            tok_l = p.op("sp", lambda e, sl=sl, t=t: e.dma_start(
                f_xg[sl][:], d_rt[128 * t:128 * t + 128, :]),
                waits=W(ffree.get("f_xg%d" % sl), scatter_prev[0]), inc=(s_x, 16))
            tok_a = p.op("pool", lambda e, sl=sl, t=t: e.dma_start(
                d_out[128 * t:128 * t + 128, :], f_xg[sl][:],
                accum_op=ALU.add), waits=[tok_l], inc=(s_out, 16))
            ffree["f_xg%d" % sl] = tok_a

        p.op("sp", None, waits=[(s_out, s_out.n, len(p.ops) - 1)])

        with nc.Block() as blk:
            @blk.sync
            def _(e):
                p.emit_stream(e, "sp")

            @blk.gpsimd
            def _(e):
                p.emit_stream(e, "pool")

            @blk.tensor
            def _(e):
                p.emit_stream(e, "pe")

            @blk.scalar
            def _(e):
                p.emit_stream(e, "act")

            @blk.vector
            def _(e):
                p.emit_stream(e, "dve")

    return nc


def make_in_maps(inputs):
    f32 = lambda a: np.ascontiguousarray(np.asarray(a, dtype=np.float32))
    x = f32(inputs["L0_attn_x"])
    wq = f32(inputs["L0_attn_Wq"])
    bq = f32(inputs["L0_attn_bq"]).reshape(1, D)
    wk = f32(inputs["L0_attn_Wk"])
    bk = f32(inputs["L0_attn_bk"]).reshape(1, 256)
    wv = f32(inputs["L0_attn_Wv"])
    bv = f32(inputs["L0_attn_bv"]).reshape(256)
    wo = f32(inputs["L0_attn_Wo"])
    bo = f32(inputs["L0_attn_bo"]).reshape(1, D)
    gate_w = f32(inputs["L0_ffn_gate_w"])
    w_gate = f32(inputs["L0_ffn_w_gate"])
    w_up = f32(inputs["L0_ffn_w_up"])
    w_down = f32(inputs["L0_ffn_w_down"])
    sw_gate = f32(inputs["L0_ffn_sw_gate"])
    sw_up = f32(inputs["L0_ffn_sw_up"])
    sw_down = f32(inputs["L0_ffn_sw_down"])

    wv_aug = np.zeros((D, 260), np.float32)
    wv_aug[:, 0:128] = wv[:, 0:128]
    wv_aug[:, 130:258] = wv[:, 128:256]
    bv_aug = np.zeros((1, 260), np.float32)
    bv_aug[0, 0:128] = bv[0:128]
    bv_aug[0, 128] = 1.0
    bv_aug[0, 130:258] = bv[128:256]
    bv_aug[0, 258] = 1.0
    gwr = np.ascontiguousarray(gate_w.reshape(16, 128, 8).transpose(1, 0, 2))
    wg = np.ascontiguousarray(np.concatenate(
        [w_gate, sw_gate[:, 0:EH][None], sw_gate[:, EH:2 * EH][None]], 0))
    wu = np.ascontiguousarray(np.concatenate(
        [w_up, sw_up[:, 0:EH][None], sw_up[:, EH:2 * EH][None]], 0))
    wd = np.ascontiguousarray(np.concatenate(
        [w_down, sw_down[0:EH, :][None], sw_down[EH:2 * EH, :][None]], 0))
    ident = np.eye(128, dtype=np.float16)
    ident32 = np.eye(128, dtype=np.float32)

    f16 = np.float16
    wq16 = np.ascontiguousarray(
        wq.reshape(16, 128, 8, 256).transpose(2, 1, 0, 3).astype(f16))
    wk16 = np.ascontiguousarray(
        wk.reshape(16, 128, 256).transpose(1, 0, 2).astype(f16))
    wv16 = np.ascontiguousarray(
        wv_aug.reshape(16, 128, 260).transpose(1, 0, 2).astype(f16))
    wo16 = np.ascontiguousarray(
        wo.reshape(16, 128, 4, 512).transpose(2, 1, 0, 3).astype(f16))
    wgp = np.concatenate([wg, np.zeros((NE, D, 128), np.float32)], axis=2)
    wup = np.concatenate([wu, np.zeros((NE, D, 128), np.float32)], axis=2)
    wg16 = np.ascontiguousarray(
        wgp.reshape(NE, 16, 128, 6, 256).transpose(0, 3, 2, 1, 4).astype(f16))
    wu16 = np.ascontiguousarray(
        wup.reshape(NE, 16, 128, 6, 256).transpose(0, 3, 2, 1, 4).astype(f16))
    wd16 = np.ascontiguousarray(
        wd.reshape(NE, 11, 128, 4, 512).transpose(0, 3, 2, 1, 4).astype(f16))

    shared = dict(wq=wq16, bq=bq, wk=wk16, bk=bk, wv_aug=wv16, bv_aug=bv_aug,
                  wo=wo16, bo=bo, gwr=gwr, wg=wg16, wu=wu16, wd=wd16,
                  ident=ident, ident32=ident32)
    in_maps = []
    kq = np.arange(512)
    for c in range(8):
        s, r = c // 4, c % 4
        perm = np.arange(16)
        if r:
            for m in range(4):
                perm[4 * m], perm[4 * m + r] = perm[4 * m + r], perm[4 * m]
        xp = np.ascontiguousarray(
            x[s].reshape(16, 512, D)[perm].reshape(SEQ, D))
        mask = np.zeros((2048, 512), np.float32)
        mask[0:512] = (kq[:, None] <= kq[None, :])
        for j in range(1, 4):
            mask[512 * j:512 * j + 512] = 1.0 if j <= r else 0.0
        m16 = mask.astype(ml_dtypes.bfloat16)
        im = dict(shared)
        im["xseq"] = xp
        im["maskT"] = m16
        in_maps.append(im)
    return in_maps


def assemble_out(results):
    out = np.empty((2, SEQ, D), np.float32)
    for c in range(8):
        o = np.asarray(results[c]["out"], np.float32)
        s, r = c // 4, c % 4
        for row0, m in ((0, 3), (512, 0), (1024, 2), (1536, 1)):
            b = 4 * m + r
            out[s, 512 * b:512 * b + 512] = o[row0:row0 + 512]
    return out


def kernel(**inputs):
    in_maps = make_in_maps(inputs)
    nc = build_program()
    res = run_bass_kernel_spmd(nc, in_maps, core_ids=list(range(8)))
    return assemble_out(res.results)


# version 15
# speedup vs baseline: 1.2720x; 1.2720x over previous
import math
from contextlib import ExitStack

import numpy as np
import ml_dtypes
import concourse.bass as bass
import concourse.mybir as mybir
from concourse.bass_utils import run_bass_kernel_spmd
from bass_rust import IndirectOffsetOnAxis as bass_IOA

F32 = mybir.dt.float32
F16 = mybir.dt.float16
BF16 = mybir.dt.bfloat16
AF = mybir.ActivationFunctionType
ALU = mybir.AluOpType
AX = mybir.AxisListType

D = 2048
NH = 16
SEQ = 8192
QB = 512
HALF_JOBS = [(3, 0), (2, 1)]
HALF_PREFIX = [8192, 6144]
NE = 10
EH = 1408
NHS = 11
SQK = 1.0 / math.sqrt(128.0)
CAP = 768
U32 = mybir.dt.uint32


class Sem:
    def __init__(self, h):
        self.h = h
        self.n = 0


class Prog:

    def __init__(self):
        self.ops = []

    def op(self, eng, emit, waits=(), inc=None):
        idx = len(self.ops)
        ws = []
        for w in waits:
            if w is None:
                continue
            s, v, widx = w
            assert widx < idx, f"wait on future token at op {idx}"
            ws.append((s, v))
        incs = [] if inc is None else [inc]
        for s, n in incs:
            s.n += n
        self.ops.append((eng, ws, emit, incs))
        if incs:
            s, _ = incs[0]
            return (s, s.n, idx)
        return (None, 0, idx)

    def emit_stream(self, eng_obj, name):
        hiwater = {}
        for eng, ws, emit, incs in self.ops:
            if eng != name:
                continue
            for s, v in ws:
                if hiwater.get(id(s), -1) >= v:
                    continue
                eng_obj.wait_ge(s.h, v)
                hiwater[id(s)] = v
            if emit is None:
                continue
            ins = emit(eng_obj)
            for s, n in incs:
                ins.then_inc(s.h, n)


def W(*toks):
    return [t for t in toks if t is not None]


def build_program(taps=()):
    nc = bass.Bass("TRN2", target_bir_lowering=False)

    d_x = nc.dram_tensor("xseq", [SEQ, D], F32, kind="ExternalInput").ap()
    d_wq = nc.dram_tensor("wq", [8, 128, 16, 256], F16, kind="ExternalInput").ap()
    d_bq = nc.dram_tensor("bq", [1, D], F32, kind="ExternalInput").ap()
    d_wk = nc.dram_tensor("wk", [128, 16, 256], F16, kind="ExternalInput").ap()
    d_bk = nc.dram_tensor("bk", [1, 256], F32, kind="ExternalInput").ap()
    d_wv = nc.dram_tensor("wv_aug", [128, 16, 260], F16, kind="ExternalInput").ap()
    d_bv = nc.dram_tensor("bv_aug", [1, 260], F32, kind="ExternalInput").ap()
    d_wo = nc.dram_tensor("wo", [4, 128, 16, 512], F16, kind="ExternalInput").ap()
    d_bo = nc.dram_tensor("bo", [1, D], F32, kind="ExternalInput").ap()
    d_gw = nc.dram_tensor("gwr", [128, 16, 8], F32, kind="ExternalInput").ap()
    d_wg = nc.dram_tensor("wg", [NE, 6, 128, 16, 256], F16,
                          kind="ExternalInput").ap()
    d_wu = nc.dram_tensor("wu", [NE, 6, 128, 16, 256], F16,
                          kind="ExternalInput").ap()
    d_wd = nc.dram_tensor("wd", [NE, 4, 128, 11, 512], F16,
                          kind="ExternalInput").ap()
    d_mask = nc.dram_tensor("maskT", [2048, QB], BF16, kind="ExternalInput").ap()
    d_xn = nc.dram_tensor("xn_sp", [2048, D], F16, kind="Internal").ap()
    d_id = nc.dram_tensor("ident", [128, 128], F16, kind="ExternalInput").ap()
    d_id32 = nc.dram_tensor("ident32", [128, 128], F32, kind="ExternalInput").ap()
    d_tbl = nc.dram_tensor("tbl", [8 * CAP + 128, 4], F32,
                           kind="ExternalOutput" if "tbl0" in taps else "Internal").ap()
    d_rt = nc.dram_tensor("rt", [8192 + 128, D], F16,
                          kind="ExternalOutput" if "rt0" in taps else "Internal").ap()
    d_rt2 = d_rt.rearrange("r (h c) -> (r h) c", h=2)
    d_out = nc.dram_tensor("out", [2048, D], F32, kind="ExternalOutput").ap()

    tap_outs = {}
    tap_dims = {
        "xn0": ([128, 16, 1024], F32), "K0": ([128, 2, 8192], F32),
        "V0": ([128, 64, 260], F32), "Q0": ([128, 16, 1024], F32),
        "mix0": ([128, 16, 1024], F32), "acc0w": ([128, 8, 2048], F32),
        "comb0": ([128, 64], F32), "acc0": ([128, 8, 2048], F32),
        "hp00": ([128, 11, 1024], F32),
    }
    for t in taps:
        if t not in tap_dims:
            continue
        shape, dt = tap_dims[t]
        tap_outs[t] = nc.dram_tensor("tap_" + t, shape, dt,
                                     kind="ExternalOutput").ap()

    p = Prog()

    with ExitStack() as st:
        E = st.enter_context
        A = nc.alloc_sbuf_tensor_at

        BASE = 16512
        g_off = [BASE]

        def galloc(name, shape, dtype):
            per = int(np.prod(shape[1:])) * mybir.dt.size(dtype)
            t = A(name, shape, dtype, offset=g_off[0])
            g_off[0] += (per + 31) // 32 * 32
            assert g_off[0] <= BASE + 22528, (name, g_off[0])
            return t

        ones = galloc("ones", [1, 512], F16)
        identS = galloc("identS", [128, 128], F16)
        bq_row = galloc("bq_row", [1, D], F16)
        bo_row = galloc("bo_row", [1, D], F16)
        bk_row = galloc("bk_row", [1, 256], F16)
        bv_row = galloc("bv_row", [1, 260], F16)
        gw_sb = galloc("gw_sb", [128, 16, 8], F16)
        qk_col = galloc("qk_col", [128, 1], F32)
        eps_c = galloc("eps_c", [128, 1], F32)
        stats = galloc("stats", [128, 24], F32)
        mv = [galloc("mv0", [128, 2], F32), galloc("mv1", [128, 2], F32)]
        rstd0 = [galloc("rstd00", [128, 1], F32), galloc("rstd01", [128, 1], F32)]
        rstd = [galloc("rstd_0", [128, 1], F32), galloc("rstd_1", [128, 1], F32)]
        rc = [galloc("rc0", [128, 1], F32), galloc("rc1", [128, 1], F32)]
        amix_tm = [galloc("amix0", [128, 128], F16), galloc("amix1", [128, 128], F16)]
        c_negm = galloc("c_negm", [128, 1], F32)
        c_e8 = galloc("c_e8", [128, 8], F32)
        c_s8 = galloc("c_s8", [128, 1], F32)
        c_rs8 = galloc("c_rs8", [128, 1], F32)
        c_pp8 = galloc("c_pp8", [128, 8], F32)
        c_m1 = galloc("c_m1", [128, 1], F32)
        c_ismax = galloc("c_ismax", [128, 8], F32)
        c_z8 = galloc("c_z8", [128, 8], F32)
        c_m2 = galloc("c_m2", [128, 1], F32)
        c_dum = galloc("c_dum", [128, 1], F32)
        combT = galloc("combT", [128, 64], F32)
        silu_t = [galloc("silu0", [128, 512], F16), galloc("silu1", [128, 512], F16)]
        id32 = galloc("id32", [128, 128], F32)
        iota_u = galloc("iota_u", [128, 1], U32)
        iota_f = galloc("iota_f", [128, 1], F32)
        trash_col = galloc("trash_col", [128, 1], F32)
        ecap8_u = galloc("ecap8_u", [8, 1], U32)
        ecap8 = galloc("ecap8", [8, 1], F32)
        carry8 = galloc("carry8", [8, 1], F32)
        t2_all = galloc("t2_all", [128, 8, 8], F32)
        ismax_all = galloc("ismax_all", [128, 8, 8], F32)
        m2nd_all = galloc("m2nd_all", [128, 8, 8], F32)
        mcols = galloc("mcols", [128, 16], F32)
        poT = galloc("poT", [128, 8], F32)
        tmp8 = galloc("tmp8", [128, 8], F32)
        dcol = [galloc("dcol0", [128, 1], F32), galloc("dcol1", [128, 1], F32)]
        dux = [galloc("du0", [128, 1], U32), galloc("du1", [128, 1], U32),
               galloc("du2", [128, 1], U32), galloc("du3", [128, 1], U32)]
        recb = [galloc("rec%d" % i, [128, 4], F32) for i in range(4)]
        sent = galloc("sent", [128, 4], F32)
        or2f = galloc("or2f", [128, 1], F32)
        or2u = galloc("or2u", [128, 1], U32)
        onec = galloc("onec", [128, 1], F16)
        ones32 = galloc("ones32", [128, 1], F32)
        rs_row = galloc("rs_row", [1, 512], F16)
        bcast_sb = galloc("bcast_sb", [128, 512], F32)

        R1 = BASE + 22528
        R2 = R1 + 32768
        R3 = R2 + 33280
        R4 = R3 + 32768
        R5 = R4 + 32768
        R6 = R5 + 32768
        assert R6 + 22784 <= 229344

        K_sb = A("K_sb", [128, 2, 8192], F16, offset=R1)
        V_sb = A("V_sb", [128, 64, 260], F16, offset=R2)
        acc = A("acc", [128, 8, D], F32, offset=R1)
        xn = A("xn", [128, 16, 1024], F16, offset=R3)
        x_t = [A("x_t0", [128, D], F32, offset=R4),
               A("x_t1", [128, D], F32, offset=R4 + 8192)]
        xn_tm = [A("xn_tm0", [128, D], F16, offset=R4 + 16384),
                 A("xn_tm1", [128, D], F16, offset=R4 + 20480)]
        xn_sc = [A("xn_sc0", [128, 16, 128], F16, offset=R4 + 24576),
                 A("xn_sc1", [128, 16, 128], F16, offset=R4 + 28672)]
        Q_sb = A("Q_sb", [128, 16, 1024], F16, offset=R4)
        wo_t = [A("wo_t0", [128, 16, 512], F16, offset=R4),
                A("wo_t1", [128, 16, 512], F16, offset=R4 + 16384)]
        wg_t = [A("wg_t0", [128, 16, 256], F16, offset=R4),
                A("wg_t1", [128, 16, 256], F16, offset=R4 + 8192)]
        wu_t = [A("wu_t0", [128, 16, 256], F16, offset=R4 + 16384),
                A("wu_t1", [128, 16, 256], F16, offset=R4 + 24576)]
        amix = A("amix", [128, 16, 1024], F16, offset=R5)
        hp = A("hp", [128, 11, 1024], F16, offset=R5)
        wk_sb = A("wk_sb", [128, 16, 256], F16, offset=R6)
        wv_sb = A("wv_sb", [128, 16, 260], F16, offset=R6 + 8192)
        wq_t = [A("wq_t0", [128, 16, 256], F16, offset=R6),
                A("wq_t1", [128, 16, 256], F16, offset=R6 + 8192)]
        mask_sb = A("mask_sb", [128, 16, 512], BF16, offset=R6)
        expT = [A("expT%d" % i, [128, 1024], BF16, offset=R6 + 16384 + 2048 * i)
                for i in range(2)]
        esum_sb = A("esum_sb", [128, 512], F32, offset=R6 + 20480)
        xw_t = [A("xw_t0", [128, D], F32, offset=R6),
                A("xw_t1", [128, D], F32, offset=R6 + 8192)]
        wd_t = [A("wd_t0", [128, 11, 512], F16, offset=R6),
                A("wd_t1", [128, 11, 512], F16, offset=R6 + 11264)]
        msk8 = A("msk8", [8, 1024], F32, offset=R6 + 17408)
        pos8 = A("pos8", [8, 1024], F32, offset=R6 + 21504)
        f_xg = [A("f_xg0", [128, D], F32, offset=R1),
                A("f_xg1", [128, D], F32, offset=R1 + 8192)]
        f_xtm = [A("f_xtm0", [128, D], F16, offset=R1 + 16384),
                 A("f_xtm1", [128, D], F16, offset=R1 + 20480)]
        f_xng = [A("f_xng0", [128, 16, 768], F16, offset=R1 + 24576),
                 A("f_xng1", [128, 16, 768], F16, offset=R1 + 49152)]
        f_hp = A("f_hp", [128, 11, 768], F16, offset=R1 + 73728)
        f_wg = [A("f_wg0", [128, 16, 256], F16, offset=R1 + 90624),
                A("f_wg1", [128, 16, 256], F16, offset=R1 + 98816)]
        f_wu = [A("f_wu0", [128, 16, 256], F16, offset=R1 + 107008),
                A("f_wu1", [128, 16, 256], F16, offset=R1 + 115200)]
        f_wd = [A("f_wd0", [128, 11, 512], F16, offset=R1 + 123392),
                A("f_wd1", [128, 11, 512], F16, offset=R1 + 134656)]
        f_rout = A("f_rout", [128, 6, 1024], F16, offset=R1 + 145920)
        f_lg = [A("f_lg", [128, 4, D], F16, offset=R1 + 170688),
                A("f_lg2", [128, 4, D], F16, offset=R1 + 90624)]
        f_rec = [A("f_rec0", [128, 4], F32, offset=R1 + 170496),
                 A("f_rec1", [128, 4], F32, offset=R1 + 170528)]
        f_xru = [A("f_xru0", [128, 1], U32, offset=R1 + 170560),
                 A("f_xru1", [128, 1], U32, offset=R1 + 170592)]
        f_orf = A("f_orf", [128, 8], F32, offset=R1 + 170624)
        f_comb = A("f_comb", [128, 8], F32, offset=R1 + 170656)
        assert R1 + 187072 <= 229344

        parena = E(nc.psum_tensor("parena", [128, 3072], F32))
        p32 = [parena[:, 512 * i:512 * i + 512] for i in range(6)]
        pair = [parena[:, 1024 * j:1024 * j + 1024] for j in range(2)]
        av_bank = parena[:, 2048:2560]
        sums_v = parena[0:1, 2560:3072]
        p16 = [E(nc.psum_tensor("pt%d" % i, [128, 1024], F16)) for i in range(2)]

        s_x = Sem(E(nc.semaphore("s_x")))
        s_w = Sem(E(nc.semaphore("s_w")))
        s_pe = Sem(E(nc.semaphore("s_pe")))
        s_act = Sem(E(nc.semaphore("s_act")))
        s_dve = Sem(E(nc.semaphore("s_dve")))
        s_out = Sem(E(nc.semaphore("s_out")))
        s_ms = Sem(E(nc.semaphore("s_ms")))
        s_sp = Sem(E(nc.semaphore("s_sp")))
        s_sc = Sem(E(nc.semaphore("s_sc")))
        s_ig = Sem(E(nc.semaphore("s_ig")))

        p.op("dve", lambda e: e.memset(ones[:], 1.0))
        p.op("dve", lambda e: e.memset(onec[:], 1.0))
        p.op("dve", lambda e: e.memset(ones32[:], 1.0))
        p.op("dve", lambda e: e.memset(qk_col[:], SQK))
        tok_init = p.op("dve", lambda e: e.memset(eps_c[:], 1e-5), inc=(s_dve, 1))

        def gdma(dst, src, waits=()):
            return p.op("pool", lambda e, d=dst, s=src: e.dma_start(d, s),
                        waits=waits, inc=(s_w, 16))

        gdma(bq_row[:], d_bq[:, :])
        gdma(bo_row[:], d_bo[:, :])
        gdma(bk_row[:], d_bk[:, :])
        gdma(bv_row[:], d_bv[:, :])
        gdma(gw_sb[:], d_gw[:, :, :])
        gdma(identS[:], d_id[:, :])
        tok_id32 = gdma(id32[:], d_id32[:, :])
        tok_iota = p.op("pool", lambda e: e.iota(iota_u[:], [[0, 1]], base=0,
                                                 channel_multiplier=1), inc=(s_w, 1))
        tok_ecap = p.op("pool", lambda e: e.iota(ecap8_u[:], [[0, 1]], base=0,
                                                 channel_multiplier=CAP), inc=(s_w, 1))
        tok_iotf = p.op("dve", lambda e: e.tensor_copy(iota_f[:], iota_u[:]),
                        waits=[tok_iota], inc=(s_dve, 1))
        tok_trc = p.op("dve", lambda e: e.tensor_scalar(
            trash_col[:], iota_f[:], float(8 * CAP), None, ALU.add),
            waits=[tok_iotf], inc=(s_dve, 1))
        tok_ec8 = p.op("dve", lambda e: e.tensor_copy(ecap8[:], ecap8_u[:]),
                       waits=[tok_ecap], inc=(s_dve, 1))
        p.op("dve", lambda e: e.memset(sent[:], 0.0))
        tok_sent = p.op("dve", lambda e: e.tensor_scalar(
            sent[:, 1:2], iota_f[:], 8192.0, None, ALU.add),
            waits=[tok_iotf], inc=(s_dve, 1))
        tok_pref = None
        for j in range(8 * CAP // 128):
            tok_pref = p.op("pool", lambda e, j=j: e.dma_start(
                d_tbl[128 * j:128 * j + 128, :], sent[:]),
                waits=[tok_sent], inc=(s_w, 16))
        tok_globals = tok_pref

        free = {}
        tap_done = set()

        def tap(name, sbuf_ap, wait_tok):
            if name in tap_outs and name not in tap_done:
                tap_done.add(name)
                p.op("pool", lambda e, n=name, a=sbuf_ap: e.dma_start(
                    tap_outs[n][...], a), waits=[wait_tok], inc=(s_out, 16))

        tpsum_free = {0: [], 1: []}

        for half in range(2):
            mlist = HALF_JOBS[half]
            prefix = HALF_PREFIX[half]
            ntiles = prefix // 128
            own_blocks = [4 * mlist[0], 4 * mlist[1]]
            own_tiles = {}
            for j, b in enumerate(own_blocks):
                for tb in range(4):
                    own_tiles[4 * b + tb] = 512 * j + 128 * tb


            tok_wk = gdma(wk_sb[:], d_wk[:, :, :],
                          waits=W(free.get("xw0"), free.get("xw1")))
            tok_wv = gdma(wv_sb[:], d_wv[:, :, :])

            kpsum_free = {0: W(free.get("gate_act")), 1: W(free.get("gate_act"))}
            vpsum_free = {0: W(free.get("gate_act")), 1: W(free.get("gate_act"))}
            kv_evacs = []
            last_xn_evac = None
            own_tiles2 = own_tiles

            def kv_stageA(i):
                sl = i % 2
                tok_x = p.op("sp", lambda e, sl=sl, i=i: e.dma_start(
                    x_t[sl][:], d_x[128 * i:128 * i + 128, :]),
                    waits=W(free.get("x_t%d" % sl), free.get("wo%d" % sl)),
                    inc=(s_x, 16))
                tok_st = None
                for k in range(4):
                    tok_st = p.op("dve", lambda e, sl=sl, k=k: e.bn_stats(
                        stats[:, 6 * k:6 * k + 6],
                        x_t[sl][:, 512 * k:512 * k + 512]),
                        waits=[tok_x] if k == 0 else (),
                        inc=(s_dve, 1) if k == 3 else None)
                tok_ag = p.op("dve", lambda e, sl=sl: e.bn_aggr(mv[sl][:], stats[:]),
                              waits=W(tok_st, free.get("mv%d" % sl)),
                              inc=(s_dve, 1))
                tok_sq = p.op("act", lambda e, sl=sl: e.activation(
                    rstd0[sl][:], mv[sl][:, 1:2], AF.Sqrt, bias=eps_c[:]),
                    waits=W(tok_ag, tok_init, free.get("rstd0%d" % sl)),
                    inc=(s_act, 1))
                free["mv%d" % sl] = tok_sq
                tok_rc = p.op("dve", lambda e, sl=sl: e.reciprocal(
                    rstd[sl][:], rstd0[sl][:]), waits=[tok_sq], inc=(s_dve, 1))
                free["rstd0%d" % sl] = tok_rc
                tok_ts = p.op("dve", lambda e, sl=sl: e.tensor_scalar(
                    xn_tm[sl][:], x_t[sl][:], mv[sl][:, 0:1], rstd[sl][:],
                    ALU.subtract, ALU.mult),
                    waits=W(tok_rc, free.get("xn_tm%d" % sl),
                            free.get("xnsp%d" % sl), free.get("wo1")),
                    inc=(s_dve, 1))
                free["x_t%d" % sl] = tok_ts
                if i in own_tiles:
                    off = own_tiles[i]
                    tok_sp = p.op("pool", lambda e, sl=sl, off=off, half=half:
                                  e.dma_start(
                        d_xn[1024 * half + off:1024 * half + off + 128, :],
                        xn_tm[sl][:]), waits=[tok_ts], inc=(s_sp, 16))
                    free["xnsp%d" % sl] = tok_sp
                    free["xnspill_last"] = tok_sp
                return tok_ts

            def kv_stageB(i, tok_ts):
                nonlocal last_xn_evac, kv_evacs
                sl = i % 2
                if i in own_tiles2:
                    off = own_tiles2[i]
                    dst_ap = lambda c0, off=off: xn[:, c0:c0 + 8, off:off + 128]
                    src_ = lambda fc, off=off: xn[:, fc, off:off + 128]
                else:
                    dst_ap = lambda c0, sl=sl: xn_sc[sl][:, c0:c0 + 8, :]
                    src_ = lambda fc, sl=sl: xn_sc[sl][:, fc, :]
                tr_toks = []
                for b16 in range(2):
                    wt = W(tok_ts, tok_globals) + tpsum_free[b16]
                    tk = None
                    for k in range(8):
                        fc = 8 * b16 + k
                        tk = p.op("pe", lambda e, sl=sl, b16=b16, k=k, fc=fc:
                                  e.transpose(p16[b16][:, 128 * k:128 * k + 128],
                                              xn_tm[sl][:, 128 * fc:128 * fc + 128],
                                              identS[:]),
                                  waits=wt if k == 0 else (),
                                  inc=(s_pe, 1) if k == 7 else None)
                    tr_toks.append(tk)
                free["xn_tm%d" % sl] = tr_toks[1]
                ev_toks = []
                for b16 in range(2):
                    wt = [tr_toks[b16]]
                    if i in own_tiles2:
                        wt = wt + W(free.get("xn_own"))
                    else:
                        wt = wt + W(free.get("wo1"))
                    ev = p.op("act", lambda e, b16=b16, dst_ap=dst_ap:
                              e.activation(
                                  dst_ap(8 * b16),
                                  p16[b16].rearrange("p (c t) -> p c t", c=8),
                                  AF.Copy),
                              waits=wt, inc=(s_act, 1))
                    tpsum_free[b16] = [ev]
                    ev_toks.append(ev)
                xn_ready = ev_toks[1]
                last_xn_evac = ev_toks[1]

                par = i % 2
                for h in range(2):
                    wt = W(ev_toks[0], tok_wk, free.get("acc_out")) + kpsum_free[par]
                    p.op("pe", lambda e, h=h, par=par: e.matmul(
                        p32[[0, 2][h] + par][:, 0:128],
                        bk_row[0:1, 128 * h:128 * h + 128], ones[0:1, 0:128],
                        start=True, stop=False), waits=wt)
                    for fc in range(8):
                        p.op("pe", lambda e, h=h, par=par, fc=fc, src_=src_:
                             e.matmul(p32[[0, 2][h] + par][:, 0:128],
                                      wk_sb[:, fc, 128 * h:128 * h + 128],
                                      src_(fc), start=False, stop=False))
                wt = W(ev_toks[0], tok_wv, free.get("acc_out")) + vpsum_free[par]
                p.op("pe", lambda e, par=par: e.matmul(
                    p32[4 + par][:, 0:260], ones[0:1, 0:128], bv_row[0:1, :],
                    start=True, stop=False), waits=wt)
                for fc in range(8):
                    p.op("pe", lambda e, par=par, fc=fc, src_=src_: e.matmul(
                        p32[4 + par][:, 0:260], src_(fc), wv_sb[:, fc, :],
                        start=False, stop=False))
                for h in range(2):
                    tk = None
                    for fc in range(8, 16):
                        tk = p.op("pe", lambda e, h=h, par=par, fc=fc, src_=src_:
                                  e.matmul(p32[[0, 2][h] + par][:, 0:128],
                                           wk_sb[:, fc, 128 * h:128 * h + 128],
                                           src_(fc), start=False, stop=(fc == 15)),
                                  waits=[ev_toks[1]] if fc == 8 else (),
                                  inc=(s_pe, 1) if fc == 15 else None)
                    tok_kev = p.op("act", lambda e, h=h, par=par, i=i: e.activation(
                        K_sb[:, h, 128 * i:128 * i + 128],
                        p32[[0, 2][h] + par][:, 0:128], AF.Copy),
                        waits=[tk], inc=(s_act, 1))
                    if h == 0:
                        kpsum_free[par] = [tok_kev]
                    else:
                        kpsum_free[par] = kpsum_free[par] + [tok_kev]
                    kv_evacs.append(tok_kev)
                tk = None
                for fc in range(8, 16):
                    tk = p.op("pe", lambda e, par=par, fc=fc, src_=src_: e.matmul(
                        p32[4 + par][:, 0:260], src_(fc), wv_sb[:, fc, :],
                        start=False, stop=(fc == 15)),
                        waits=[ev_toks[1]] if fc == 8 else (),
                        inc=(s_pe, 1) if fc == 15 else None)
                tok_vev = p.op("dve", lambda e, par=par, i=i: e.tensor_copy(
                    V_sb[:, i, :], p32[4 + par][:, 0:260]),
                    waits=[tk] + W(free.get("acc_out")), inc=(s_dve, 1))
                vpsum_free[par] = [tok_vev]
                kv_evacs.append(tok_vev)
                return tok_vev

            ts_tok = {0: kv_stageA(0)}
            for i in range(ntiles):
                if i + 1 < ntiles:
                    ts_tok[i + 1] = kv_stageA(i + 1)
                tok_vev = kv_stageB(i, ts_tok.pop(i))
            free["xn_own"] = None
            kv_last_dve = tok_vev
            kv_last_act = kv_evacs[-2]
            if half == 0:
                tap("xn0", xn[:, :, :], last_xn_evac)
                tap("K0", K_sb[:, :, :], kv_last_act)
                tap("V0", V_sb[:, :, :], kv_last_dve)

            qbank_free = {k: kpsum_free[0] + kpsum_free[1] for k in range(4)}
            qbank_free[4] = vpsum_free[0]
            qbank_free[5] = vpsum_free[1]
            q_done = None
            bk_ctr = 0
            for qc in range(8):
                sl = qc % 2
                wqw = W(free.get("wq%d" % sl))
                if qc < 2:
                    wqw = wqw + [kv_last_dve]
                tok_wq = gdma(wq_t[sl][:], d_wq[qc, :, :, :], waits=wqw)
                for hl in range(2):
                    hd = 2 * qc + hl
                    for tg in range(2):
                        bk = bk_ctr % 4
                        bk_ctr += 1
                        wt = W(tok_wq, kv_last_dve, kv_last_act) + qbank_free[bk]
                        p.op("pe", lambda e, hd=hd, bk=bk: e.matmul(
                            p32[bk][:, 0:512],
                            bq_row[0:1, 128 * hd:128 * hd + 128],
                            ones[0:1, 0:512], start=True, stop=False), waits=wt)
                        tk = None
                        for fc in range(16):
                            tk = p.op("pe", lambda e, bk=bk, fc=fc, sl=sl, hl=hl,
                                      tg=tg: e.matmul(
                                p32[bk][:, 0:512],
                                wq_t[sl][:, fc, 128 * hl:128 * hl + 128],
                                xn[:, fc, 512 * tg:512 * tg + 512],
                                start=False, stop=(fc == 15)),
                                inc=(s_pe, 1) if fc == 15 else None)
                        free["wq%d" % sl] = tk
                        tok_qe = p.op("act", lambda e, hd=hd, bk=bk, tg=tg:
                                      e.activation(
                                          Q_sb[:, hd, 512 * tg:512 * tg + 512],
                                          p32[bk][:, 0:512], AF.Copy,
                                          scale=qk_col[:]),
                                      waits=[tk, tok_init], inc=(s_act, 1))
                        qbank_free[bk] = [tok_qe]
            q_last = tok_qe
            if half == 0:
                tap("Q0", Q_sb[:, :, :], q_last)

            tok_mask = gdma(mask_sb[:],
                            d_mask.rearrange("(c p) q -> p c q", p=128),
                            waits=W(free.get("wq0"), free.get("wq1")))
            pair_free = {0: qbank_free[0] + qbank_free[1],
                         1: qbank_free[2] + qbank_free[3]}
            av_free = vpsum_free[0]
            sums_free = vpsum_free[1]
            expT_free = {k: [] for k in range(2)}
            esum_free = []
            tok_me = None
            for slot in range(2):
                m = mlist[slot]
                kcs = (2048 * (m + 1)) // 128
                qb0 = 512 * slot
                for hd in range(NH):
                    kvh = hd // 8
                    es = esum_sb
                    tok_av_last = None
                    es_prev = None
                    pend = None

                    def emit_av(tok_rdy, kp, tok_es, kvh=kvh, kcs=kcs):
                        nonlocal tok_av_last
                        eb = kp % 2
                        tk = None
                        for half_k in range(2):
                            kc = 2 * kp + half_k
                            wt2 = [tok_rdy]
                            if kc == 0:
                                wt2 = wt2 + av_free
                            tk = p.op("pe", lambda e, eb=eb, kc=kc, kvh=kvh,
                                      hk=half_k, kcs=kcs: e.matmul(
                                av_bank[:, 0:512],
                                V_sb[:, kc, 130 * kvh:130 * kvh + 128],
                                expT[eb][:, 512 * hk:512 * hk + 512],
                                start=(kc == 0), stop=(kc == kcs - 1)),
                                waits=wt2,
                                inc=(s_pe, 1) if half_k == 1 else None)
                        if kp % 2 == 0:
                            tks = None
                            for half_k in range(2):
                                wt3 = sums_free if kp == 0 and half_k == 0 else ()
                                tks = p.op("pe", lambda e, eb=eb, hk=half_k,
                                           kp=kp: e.matmul(
                                    sums_v, onec[:, 0:1],
                                    expT[eb][:, 512 * hk:512 * hk + 512],
                                    start=(kp == 0 and hk == 0), stop=False),
                                    waits=wt3,
                                    inc=(s_pe, 1) if half_k == 1 else None)
                            expT_free[eb] = [tks]
                        else:
                            expT_free[eb] = [tk, tok_es]
                        if kc == kcs - 1:
                            tok_av_last = tk

                    for kp in range(kcs // 2):
                        pb = kp % 2
                        eb = kp % 2
                        wt = W(q_last, kv_last_dve) + pair_free[pb]
                        for half_k in range(2):
                            kc = 2 * kp + half_k
                            tok_qk = p.op("pe", lambda e, kvh=kvh, hd=hd, kc=kc,
                                          pb=pb, hk=half_k, qb0=qb0: e.matmul(
                                pair[pb][:, 512 * hk:512 * hk + 512],
                                K_sb[:, kvh, 128 * kc:128 * kc + 128],
                                Q_sb[:, hd, qb0:qb0 + 512], start=True, stop=True),
                                waits=wt if half_k == 0 else (),
                                inc=(s_pe, 1) if half_k == 1 else None)
                        tok_ex = p.op("act", lambda e, pb=pb, eb=eb: e.activation(
                            expT[eb][:], pair[pb][:, 0:1024], AF.Exp),
                            waits=[tok_qk] + expT_free[eb], inc=(s_act, 1))
                        pair_free[pb] = [tok_ex]
                        tok_rdy = tok_ex
                        if 2 * kp >= kcs - 16:
                            mi = 2 * kp - (kcs - 16)
                            tok_rdy = p.op("dve", lambda e, eb=eb, mi=mi:
                                           e.tensor_tensor(
                                               expT[eb][:], expT[eb][:],
                                               mask_sb[:, mi:mi + 2, :]
                                               .rearrange("p c q -> p (c q)"),
                                               ALU.mult),
                                           waits=[tok_ex, tok_mask],
                                           inc=(s_dve, 1))
                        if kp % 2 == 1:
                            if kp == 1:
                                tok_es = p.op("dve", lambda e, eb=eb, es=es:
                                              e.tensor_tensor(
                                                  es[:], expT[eb][:, 0:512],
                                                  expT[eb][:, 512:1024],
                                                  ALU.add),
                                              waits=[tok_rdy] + esum_free,
                                              inc=(s_dve, 1))
                            else:
                                t1 = p.op("dve", lambda e, eb=eb, es=es:
                                          e.tensor_tensor(
                                              es[:], es[:], expT[eb][:, 0:512],
                                              ALU.add),
                                          waits=W(tok_rdy, es_prev),
                                          inc=(s_dve, 1))
                                tok_es = p.op("dve", lambda e, eb=eb, es=es:
                                              e.tensor_tensor(
                                                  es[:], es[:],
                                                  expT[eb][:, 512:1024],
                                                  ALU.add),
                                              waits=[t1], inc=(s_dve, 1))
                            es_prev = tok_es
                        else:
                            tok_es = None
                        if pend is not None:
                            emit_av(*pend)
                        pend = (tok_rdy, kp, tok_es)
                    emit_av(*pend)
                    tok_sums = p.op("pe", lambda e, es=es: e.matmul(
                        sums_v, ones32[:, 0:1], es[:], start=False, stop=True),
                        waits=[es_prev], inc=(s_pe, 1))
                    esum_free = [tok_sums]
                    tok_cp = p.op("act", lambda e: e.activation(
                        rs_row[:], sums_v, AF.Copy),
                        waits=W(tok_sums, free.get("rs_row")), inc=(s_act, 1))
                    sums_free = [tok_cp]
                    bcb = (kcs // 2) % 2
                    tok_bc = p.op("pe", lambda e, bcb=bcb: e.matmul(
                        pair[bcb][:, 0:512], ones[0:1, 0:128], rs_row[:],
                        start=True, stop=True),
                        waits=[tok_cp] + pair_free[bcb], inc=(s_pe, 1))
                    free["rs_row"] = tok_bc
                    tok_rc2 = p.op("dve", lambda e, bcb=bcb: e.reciprocal(
                        bcast_sb[:], pair[bcb][:, 0:512]),
                        waits=W(tok_bc, free.get("bcast_sb")), inc=(s_dve, 1))
                    pair_free[bcb] = [tok_rc2]
                    tok_me = p.op("dve", lambda e, hd=hd, qb0=qb0:
                                  e.tensor_tensor(
                                      amix[:, hd, qb0:qb0 + 512],
                                      av_bank[:, 0:512], bcast_sb[:],
                                      ALU.mult),
                                  waits=W(tok_av_last, tok_rc2), inc=(s_dve, 1))
                    free["bcast_sb"] = tok_me
                    av_free = [tok_me]
            mix_last = tok_me
            if half == 0:
                tap("mix0", amix[:, :, :], mix_last)

            wo_bank_free = {k: [] for k in range(6)}
            wo_pe_last = None
            tok_ae = None
            for dmc in range(4):
                sl = dmc % 2
                tok_wo = gdma(wo_t[sl][:], d_wo[dmc, :, :, :],
                              waits=W(free.get("wo%d" % sl), mix_last))
                for t8 in range(8):
                    bk = (dmc * 8 + t8) % 6
                    wt = W(tok_wo, mix_last) + wo_bank_free[bk]
                    p.op("pe", lambda e, dmc=dmc, bk=bk: e.matmul(
                        p32[bk][:, 0:512], ones[0:1, 0:128],
                        bo_row[0:1, 512 * dmc:512 * dmc + 512],
                        start=True, stop=False), waits=wt)
                    tk = None
                    for fc in range(16):
                        tk = p.op("pe", lambda e, bk=bk, fc=fc, sl=sl, t8=t8:
                                  e.matmul(p32[bk][:, 0:512],
                                           amix[:, fc, 128 * t8:128 * t8 + 128],
                                           wo_t[sl][:, fc, :],
                                           start=False, stop=(fc == 15)),
                                  inc=(s_pe, 1) if fc == 15 else None)
                    free["wo%d" % sl] = tk
                    wo_pe_last = tk
                    tok_ae = p.op("dve", lambda e, bk=bk, t8=t8, dmc=dmc:
                                  e.tensor_copy(
                                      acc[:, t8, 512 * dmc:512 * dmc + 512],
                                      p32[bk][:, 0:512]),
                                  waits=[tk], inc=(s_dve, 1))
                    wo_bank_free[bk] = [tok_ae]
            acc_w_last = tok_ae
            tok_ax = None
            for t8 in range(8):
                sl = t8 % 2
                j, tb = t8 // 4, t8 % 4
                row0 = 512 * own_blocks[j] + 128 * tb
                tok_xw = p.op("sp", lambda e, sl=sl, row0=row0: e.dma_start(
                    xw_t[sl][:], d_x[row0:row0 + 128, :]),
                    waits=W(free.get("xw%d" % sl), mix_last), inc=(s_x, 16))
                tok_ax = p.op("dve", lambda e, sl=sl, t8=t8: e.tensor_tensor(
                    acc[:, t8, :], acc[:, t8, :], xw_t[sl][:], ALU.add),
                    waits=[tok_xw, acc_w_last], inc=(s_dve, 1))
                free["xw%d" % sl] = tok_ax
            accx_last = tok_ax
            if half == 0:
                tap("acc0w", acc[:, :, :], accx_last)

            comb_prev = None
            gate_pe_last = None
            gate_act_last = None
            msk_ev_last = None
            for t8 in range(8):
                bk = t8 % 2
                wt = W(tok_globals, comb_prev) + wo_bank_free[bk]
                tk = None
                for fc in range(16):
                    tk = p.op("pe", lambda e, bk=bk, fc=fc, t8=t8: e.matmul(
                        p32[bk][:, 0:8], xn[:, fc, 128 * t8:128 * t8 + 128],
                        gw_sb[:, fc, :], start=(fc == 0), stop=(fc == 15)),
                        waits=wt if fc == 0 else (),
                        inc=(s_pe, 1) if fc == 15 else None)
                gate_pe_last = tk
                tok_nm = p.op("dve", lambda e, bk=bk: e.reduce_max(
                    c_negm[:], p32[bk][:, 0:8], AX.X, negate=True),
                    waits=W(tk, comb_prev), inc=(s_dve, 1))
                tok_e8 = p.op("act", lambda e, bk=bk: e.activation(
                    c_e8[:], p32[bk][:, 0:8], AF.Exp, bias=c_negm[:],
                    accum_out=c_s8[:]), waits=[tok_nm], inc=(s_act, 1))
                tok_e8b = p.op("act", lambda e: e.activation(
                    c_dum[:], c_negm[:], AF.Square), waits=[tok_e8],
                    inc=(s_act, 1))
                gate_act_last = tok_e8b
                wo_bank_free[bk] = [tok_e8b]
                tok_rs = p.op("dve", lambda e: e.reciprocal(c_rs8[:], c_s8[:]),
                              waits=[tok_e8b], inc=(s_dve, 1))
                tok_pp = p.op("dve", lambda e: e.tensor_scalar(
                    c_pp8[:], c_e8[:], c_rs8[:], None, ALU.mult),
                    waits=[tok_rs], inc=(s_dve, 1))
                tok_m1 = p.op("dve", lambda e: e.reduce_max(
                    c_m1[:], c_pp8[:], AX.X), waits=[tok_pp], inc=(s_dve, 1))
                tok_im = p.op("dve", lambda e, t8=t8: e.tensor_scalar(
                    ismax_all[:, t8, :], c_pp8[:], c_m1[:], None, ALU.is_ge),
                    waits=[tok_m1], inc=(s_dve, 1))
                tok_z8 = p.op("dve", lambda e, t8=t8: e.scalar_tensor_tensor(
                    c_z8[:], ismax_all[:, t8, :], -1e30, c_pp8[:],
                    ALU.mult, ALU.add), waits=[tok_im], inc=(s_dve, 1))
                tok_m2 = p.op("dve", lambda e: e.reduce_max(
                    c_m2[:], c_z8[:], AX.X), waits=[tok_z8], inc=(s_dve, 1))
                tok_t2 = p.op("dve", lambda e, t8=t8: e.tensor_scalar(
                    t2_all[:, t8, :], c_pp8[:], c_m2[:], None, ALU.is_ge),
                    waits=[tok_m2], inc=(s_dve, 1))
                tok_ms = p.op("dve", lambda e, t8=t8: e.tensor_sub(
                    m2nd_all[:, t8, :], t2_all[:, t8, :], ismax_all[:, t8, :]),
                    waits=[tok_t2], inc=(s_dve, 1))
                tok_c1 = p.op("dve", lambda e, t8=t8: e.tensor_copy(
                    mcols[:, 2 * t8:2 * t8 + 1], c_m1[:]), waits=[tok_ms],
                    inc=(s_dve, 1))
                tok_c2 = p.op("dve", lambda e, t8=t8: e.tensor_copy(
                    mcols[:, 2 * t8 + 1:2 * t8 + 2], c_m2[:]), waits=[tok_c1],
                    inc=(s_dve, 1))
                comb_prev = tok_c2
                tbk = 2 + t8 % 2
                tok_mtr = p.op("pe", lambda e, t8=t8, tbk=tbk: e.transpose(
                    p32[tbk][0:8, 0:128], t2_all[:, t8, :], id32[:]),
                    waits=W(tok_ms, tok_id32) + wo_bank_free.get(tbk, []),
                    inc=(s_pe, 1))
                tok_mev = p.op("dve", lambda e, t8=t8, tbk=tbk: e.tensor_copy(
                    msk8[:, 128 * t8:128 * t8 + 128], p32[tbk][0:8, 0:128]),
                    waits=[tok_mtr], inc=(s_dve, 1))
                wo_bank_free[tbk] = [tok_mev]
                msk_ev_last = tok_mev
            free["gate_act"] = gate_act_last
            free["xn_own"] = gate_pe_last

            if half == 0:
                tok_scan = p.op("dve", lambda e: e.tensor_tensor_scan(
                    pos8[:], msk8[:], msk8[:], 0.0, ALU.add, ALU.bypass),
                    waits=[msk_ev_last], inc=(s_dve, 1))
            else:
                tok_scan = p.op("dve", lambda e: e.tensor_tensor_scan(
                    pos8[:], msk8[:], msk8[:], carry8[:], ALU.add, ALU.bypass),
                    waits=[msk_ev_last], inc=(s_dve, 1))
            tok_cy = p.op("dve", lambda e: e.tensor_copy(
                carry8[:], pos8[:, 1023:1024]), waits=[tok_scan], inc=(s_dve, 1))
            tok_px = p.op("dve", lambda e: e.tensor_sub(
                pos8[:], pos8[:], msk8[:]), waits=[tok_cy], inc=(s_dve, 1))
            tok_ov = p.op("dve", lambda e: e.tensor_scalar(
                msk8[:], pos8[:], float(CAP), None, ALU.is_ge),
                waits=[tok_px], inc=(s_dve, 1))
            tok_po = p.op("dve", lambda e: e.scalar_tensor_tensor(
                pos8[:], msk8[:], float(16 * CAP), pos8[:], ALU.mult, ALU.add),
                waits=[tok_ov], inc=(s_dve, 1))
            tok_po2 = p.op("dve", lambda e: e.tensor_scalar(
                pos8[:], pos8[:], ecap8[:], None, ALU.add),
                waits=W(tok_po, tok_ec8), inc=(s_dve, 1))

            rec_sc_last = None
            for t8 in range(8):
                tbk = 2 + t8 % 2
                tok_ptr = p.op("pe", lambda e, t8=t8, tbk=tbk: e.transpose(
                    p32[tbk][:, 0:8], pos8[0:8, 128 * t8:128 * t8 + 128],
                    id32[0:8, 0:8]),
                    waits=W(tok_po2) + wo_bank_free[tbk], inc=(s_pe, 1))
                tok_pev = p.op("dve", lambda e, tbk=tbk: e.tensor_copy(
                    poT[:], p32[tbk][:, 0:8]), waits=W(
                        tok_ptr, free.get("poT")), inc=(s_dve, 1))
                wo_bank_free[tbk] = [tok_pev]
                j, tb = t8 // 4, t8 % 4
                xrow0 = float(512 * own_blocks[j] + 128 * tb)
                orow0 = float(1024 * half + 128 * t8)
                tok_leg = None
                for leg in range(2):
                    lm = ismax_all if leg == 0 else m2nd_all
                    rr = recb[2 * (t8 % 2) + leg]
                    du = dux[2 * (t8 % 2) + leg]
                    tok_mu = p.op("dve", lambda e, lm=lm, t8=t8: e.tensor_mul(
                        tmp8[:], lm[:, t8, :], poT[:]),
                        waits=W(tok_pev, free.get("tmp8")), inc=(s_dve, 1))
                    tok_d = p.op("dve", lambda e, leg=leg: e.reduce_sum(
                        dcol[leg][:], tmp8[:], AX.X), waits=[tok_mu],
                        inc=(s_dve, 1))
                    free["tmp8"] = tok_d
                    tok_dc = p.op("dve", lambda e, leg=leg: e.tensor_tensor(
                        dcol[leg][:], dcol[leg][:], trash_col[:], ALU.min),
                        waits=W(tok_d, tok_trc), inc=(s_dve, 1))
                    tok_du = p.op("dve", lambda e, leg=leg, du=du: e.tensor_copy(
                        du[:], dcol[leg][:]),
                        waits=W(tok_dc, free.get(du.name)), inc=(s_dve, 1))
                    tok_r0 = p.op("dve", lambda e, rr=rr, xrow0=xrow0:
                                  e.tensor_scalar(rr[:, 0:1], iota_f[:], xrow0,
                                                  None, ALU.add),
                                  waits=W(free.get(rr.name), tok_iotf),
                                  inc=(s_dve, 1))
                    orow_leg = orow0 + 2048.0 * leg
                    tok_r1 = p.op("dve", lambda e, rr=rr, orow_leg=orow_leg:
                                  e.tensor_scalar(rr[:, 1:2], iota_f[:],
                                                  orow_leg, None, ALU.add),
                                  waits=[tok_r0], inc=(s_dve, 1))
                    tok_r2 = p.op("dve", lambda e, rr=rr, t8=t8, leg=leg:
                                  e.tensor_copy(rr[:, 2:3],
                                                mcols[:, 2 * t8 + leg:2 * t8 + leg + 1]),
                                  waits=[tok_r1], inc=(s_dve, 1))
                    tok_r3 = p.op("dve", lambda e, rr=rr, orow0=orow0:
                                  e.tensor_scalar(rr[:, 3:4], iota_f[:], orow0,
                                                  None, ALU.add),
                                  waits=[tok_r2], inc=(s_dve, 1))
                    tok_sc = p.op("pool", lambda e, rr=rr, du=du:
                                  e.indirect_dma_start(
                                      d_tbl[:, :], bass_IOA(du[:, 0:1], 0),
                                      rr[:], None),
                                  waits=W(tok_du, tok_r3, tok_globals),
                                  inc=(s_ms, 16))
                    free[rr.name] = tok_sc
                    free[du.name] = tok_sc
                    rec_sc_last = tok_sc
                free["poT"] = tok_mu
            free["rec_scatter_h%d" % half] = rec_sc_last

            out_view = d_out[1024 * half:1024 * half + 1024, :] \
                .rearrange("(t p) d -> p t d", p=128)
            tok_out = p.op("sp", lambda e, ov=out_view: e.dma_start(ov, acc[:]),
                           waits=[accx_last], inc=(s_out, 16))
            free["acc_out"] = tok_out
            free["x_t0"] = None
            free["x_t1"] = None


        ffn_bar = (s_out, s_out.n, len(p.ops) - 1)
        spills_done = free["xnspill_last"]
        for eng in ("sp", "pool", "pe", "act", "dve"):
            p.op(eng, None, waits=[ffn_bar, spills_done])
        recs_done = free["rec_scatter_h1"]

        OWN_ALL = [12, 0, 8, 4]

        def xrow_of(th):
            return 512 * OWN_ALL[th // 4] + 128 * (th % 4)

        ffree = {}
        fbank = {k: [] for k in range(6)}
        ftp = {0: [], 1: []}
        scatter_prev = [None]

        jobs = [("direct", 8, 6, 0, "init"), ("direct", 8, 6, 6, "init"),
                ("direct", 8, 4, 12, "init"), ("direct", 9, 6, 0, "add"),
                ("direct", 9, 6, 6, "add"), ("direct", 9, 4, 12, "add")]
        for e_i in range(8):
            jobs.append(("gather", e_i, 6, 0, "scat"))

        for jn, (kind, e_i, ntl, tile0, mode) in enumerate(jobs):
            buf = jn % 2
            xng = f_xng[buf]

            def f_stageA(j, kind=kind, e_i=e_i, tile0=tile0):
                sl = j % 2
                if kind == "gather":
                    tok_rec = p.op("sp", lambda e, sl=sl, j=j, e_i=e_i:
                                   e.dma_start(f_rec[sl][:],
                                               d_tbl[e_i * CAP + 128 * j:
                                                     e_i * CAP + 128 * j + 128, :]),
                                   waits=W(ffree.get("rec%d" % sl), recs_done),
                                   inc=(s_x, 16))
                    tok_xr = p.op("dve", lambda e, sl=sl: e.tensor_copy(
                        f_xru[sl][:], f_rec[sl][:, 3:4]),
                        waits=W(tok_rec, ffree.get("xru%d" % sl)),
                        inc=(s_dve, 1))
                    tok_cm = p.op("dve", lambda e, sl=sl, j=j: e.tensor_copy(
                        f_comb[:, j:j + 1], f_rec[sl][:, 2:3]),
                        waits=[tok_xr], inc=(s_dve, 1))
                    tok_of = p.op("dve", lambda e, sl=sl, j=j: e.tensor_copy(
                        f_orf[:, j:j + 1], f_rec[sl][:, 1:2]),
                        waits=[tok_cm], inc=(s_dve, 1))
                    ffree["rec%d" % sl] = tok_of
                    tok_x = p.op("pool", lambda e, sl=sl: e.indirect_dma_start(
                        f_xtm[sl][:], None, d_xn[:, :],
                        bass_IOA(f_xru[sl][:, 0:1], 0)),
                        waits=W(tok_of, ffree.get("xtm%d" % sl)),
                        inc=(s_ig, 16))
                    ffree["xru%d" % sl] = tok_x
                else:
                    r0 = 128 * (tile0 + j)
                    tok_x = p.op("sp", lambda e, sl=sl, r0=r0: e.dma_start(
                        f_xtm[sl][:], d_xn[r0:r0 + 128, :]),
                        waits=W(ffree.get("xtm%d" % sl), recs_done),
                        inc=(s_x, 16))
                return tok_x

            def f_stageB(j, tok_ts, buf=buf):
                sl = j % 2
                tr_toks = []
                for b16 in range(2):
                    wt = W(tok_ts, tok_globals) + ftp[b16]
                    tk = None
                    for k in range(8):
                        fc = 8 * b16 + k
                        tk = p.op("pe", lambda e, sl=sl, b16=b16, k=k, fc=fc:
                                  e.transpose(p16[b16][:, 128 * k:128 * k + 128],
                                              f_xtm[sl][:, 128 * fc:128 * fc + 128],
                                              identS[:]),
                                  waits=wt if k == 0 else (),
                                  inc=(s_pe, 1) if k == 7 else None)
                    tr_toks.append(tk)
                ffree["xtm%d" % sl] = tr_toks[1]
                evl = None
                for b16 in range(2):
                    evl = p.op("act", lambda e, b16=b16, j=j, buf=buf:
                               e.activation(
                                   f_xng[buf][:, 8 * b16:8 * b16 + 8,
                                              128 * j:128 * j + 128],
                                   p16[b16].rearrange("p (c t) -> p c t", c=8),
                                   AF.Copy),
                               waits=[tr_toks[b16]] + W(ffree.get("xng%d" % buf)),
                               inc=(s_act, 1))
                    ftp[b16] = [evl]
                return evl

            ts_prev = f_stageA(0)
            xng_ready = None
            for j in range(ntl):
                ts_next = f_stageA(j + 1) if j + 1 < ntl else None
                xng_ready = f_stageB(j, ts_prev)
                ts_prev = ts_next

            tgroups = [(0, 512), (512, 256)] if ntl == 6 else [(0, 512)]
            hcl = [(256 * c, 256 if c < 5 else 128) for c in range(6)]
            hp_free_l = ffree.get("hp", [])
            tok_hp = None
            for ci, (hoff, cw) in enumerate(hcl):
                sl = ci % 2
                tok_wg = gdma(f_wg[sl][:, :, :], d_wg[e_i, ci, :, :, :],
                              waits=W(ffree.get("fwg%d" % sl)))
                tok_wu = gdma(f_wu[sl][:, :, :], d_wu[e_i, ci, :, :, :],
                              waits=W(ffree.get("fwu%d" % sl)))
                for hs in range(cw // 128):
                    hglob = hoff // 128 + hs
                    for tg, (t0, tw) in enumerate(tgroups):
                        gb = 2 * ((hglob + tg) % 2)
                        spr = (hglob + tg) % 2
                        wt = W(tok_wg, xng_ready) + fbank[gb]
                        tkg = None
                        for fc in range(16):
                            tkg = p.op("pe", lambda e, gb=gb, fc=fc, sl=sl,
                                       hs=hs, t0=t0, tw=tw, buf=buf: e.matmul(
                                p32[gb][:, 0:tw],
                                f_wg[sl][:, fc, 128 * hs:128 * hs + 128],
                                f_xng[buf][:, fc, t0:t0 + tw],
                                start=(fc == 0), stop=(fc == 15)),
                                waits=wt if fc == 0 else (),
                                inc=(s_pe, 1) if fc == 15 else None)
                        wt = W(tok_wu) + fbank[gb + 1]
                        tku = None
                        for fc in range(16):
                            tku = p.op("pe", lambda e, gb=gb, fc=fc, sl=sl,
                                       hs=hs, t0=t0, tw=tw, buf=buf: e.matmul(
                                p32[gb + 1][:, 0:tw],
                                f_wu[sl][:, fc, 128 * hs:128 * hs + 128],
                                f_xng[buf][:, fc, t0:t0 + tw],
                                start=(fc == 0), stop=(fc == 15)),
                                waits=wt if fc == 0 else (),
                                inc=(s_pe, 1) if fc == 15 else None)
                        ffree["fwg%d" % sl] = tkg
                        ffree["fwu%d" % sl] = tku
                        tok_si = p.op("act", lambda e, gb=gb, spr=spr, tw=tw:
                                      e.activation(silu_t[spr][:, 0:tw],
                                                   p32[gb][:, 0:tw], AF.Silu),
                                      waits=W(tkg, free.get("silu%d" % spr)),
                                      inc=(s_act, 1))
                        fbank[gb] = [tok_si]
                        tok_hp = p.op("dve", lambda e, hglob=hglob, t0=t0, tw=tw,
                                      gb=gb, spr=spr: e.tensor_tensor(
                            f_hp[:, hglob, t0:t0 + tw],
                            silu_t[spr][:, 0:tw], p32[gb + 1][:, 0:tw], ALU.mult),
                            waits=W(tku, tok_si) + hp_free_l, inc=(s_dve, 1))
                        free["silu%d" % spr] = tok_hp
                        fbank[gb + 1] = [tok_hp]
            hp_ready = tok_hp
            ffree["xng%d" % buf] = tok_hp

            hp_pe_last = None
            job_sc = []
            for dmh in range(2):
                ev_toks = {}
                for q in range(2):
                    dmc = 2 * dmh + q
                    sl = dmc % 2
                    tok_wd = gdma(f_wd[sl][:], d_wd[e_i, dmc, :, :, :],
                                  waits=W(ffree.get("fwd%d" % sl)))
                    for t in range(ntl):
                        bk = 4 + (t % 2)
                        wt = W(tok_wd, hp_ready) + fbank[bk]
                        if dmh == 1 and q == 0:
                            wt = wt + [job_sc[t]]
                        tk = None
                        for hs in range(NHS):
                            tk = p.op("pe", lambda e, bk=bk, hs=hs, t=t, sl=sl:
                                      e.matmul(p32[bk][:, 0:512],
                                               f_hp[:, hs, 128 * t:128 * t + 128],
                                               f_wd[sl][:, hs, 0:512],
                                               start=(hs == 0),
                                               stop=(hs == NHS - 1)),
                                      waits=wt if hs == 0 else (),
                                      inc=(s_pe, 1) if hs == NHS - 1 else None)
                        hp_pe_last = tk
                        ffree["fwd%d" % sl] = tk
                        if kind == "gather":
                            tok_ev = p.op("dve", lambda e, bk=bk, t=t, q=q:
                                          e.scalar_tensor_tensor(
                                f_rout[:, t, 512 * q:512 * q + 512],
                                p32[bk][:, 0:512], f_comb[:, t:t + 1],
                                f_rout[:, t, 512 * q:512 * q + 512],
                                ALU.mult, ALU.bypass), waits=[tk],
                                inc=(s_dve, 1))
                        else:
                            tok_ev = p.op("dve", lambda e, bk=bk, t=t, q=q:
                                          e.tensor_copy(
                                f_rout[:, t, 512 * q:512 * q + 512],
                                p32[bk][:, 0:512]), waits=[tk], inc=(s_dve, 1))
                        fbank[bk] = [tok_ev]
                        ev_toks.setdefault(t, []).append(tok_ev)
                for t in range(ntl):
                    if kind == "gather":
                        t2f = p.op("dve", lambda e, t=t, dmh=dmh: e.tensor_scalar(
                            or2f[:], f_orf[:, t:t + 1], 2.0, float(dmh),
                            ALU.mult, ALU.add),
                            waits=ev_toks[t] + W(ffree.get("or2")), inc=(s_dve, 1))
                        t2u = p.op("dve", lambda e: e.tensor_copy(or2u[:], or2f[:]),
                                   waits=[t2f], inc=(s_dve, 1))
                        tok_sc = p.op("pool", lambda e, t=t: e.indirect_dma_start(
                            d_rt2[:, :], bass_IOA(or2u[:, 0:1], 0),
                            f_rout[:, t, :], None, oob_is_err=True),
                            waits=[t2u], inc=(s_sc, 16))
                        ffree["or2"] = tok_sc
                    else:
                        r0 = 4096 + 2048 * (e_i - 8) + 128 * (tile0 + t)
                        tok_sc = p.op("pool", lambda e, t=t, r0=r0, dmh=dmh:
                                      e.dma_start(
                            d_rt[r0:r0 + 128, 1024 * dmh:1024 * dmh + 1024],
                            f_rout[:, t, :]),
                            waits=ev_toks[t], inc=(s_sc, 16))
                    job_sc.append(tok_sc)
            scatter_prev[0] = job_sc[-1]
            ffree["hp"] = [hp_pe_last]

        for t in range(16):
            sl = t % 2
            lg = f_lg[sl]
            ltoks = []
            for leg in range(4):
                q = "sp" if leg % 2 == 0 else "pool"
                ltoks.append(p.op(q, lambda e, lg=lg, leg=leg, t=t: e.dma_start(
                    lg[:, leg, :],
                    d_rt[2048 * leg + 128 * t:2048 * leg + 128 * t + 128, :]),
                    waits=W(ffree.get("lg%d_%d" % (sl, leg)), scatter_prev[0]),
                    inc=(s_x, 16) if leg % 2 == 0 else (s_w, 16)))
            tok_s = p.op("dve", lambda e, sl=sl, lg=lg: e.tensor_copy(
                f_xg[sl][:], lg[:, 0, :]),
                waits=W(ltoks[0], ffree.get("f_xg%d" % sl)), inc=(s_dve, 1))
            ffree["lg%d_0" % sl] = tok_s
            for leg in range(1, 4):
                tok_s = p.op("dve", lambda e, sl=sl, lg=lg, leg=leg:
                             e.tensor_tensor(
                    f_xg[sl][:], f_xg[sl][:], lg[:, leg, :], ALU.add),
                    waits=[tok_s, ltoks[leg]], inc=(s_dve, 1))
                ffree["lg%d_%d" % (sl, leg)] = tok_s
            tok_a = p.op("pool", lambda e, sl=sl, t=t: e.dma_start(
                d_out[128 * t:128 * t + 128, :], f_xg[sl][:],
                accum_op=ALU.add), waits=[tok_s], inc=(s_out, 16))
            ffree["f_xg%d" % sl] = tok_a

        p.op("sp", None, waits=[(s_out, s_out.n, len(p.ops) - 1)])

        with nc.Block() as blk:
            @blk.sync
            def _(e):
                p.emit_stream(e, "sp")

            @blk.gpsimd
            def _(e):
                p.emit_stream(e, "pool")

            @blk.tensor
            def _(e):
                p.emit_stream(e, "pe")

            @blk.scalar
            def _(e):
                p.emit_stream(e, "act")

            @blk.vector
            def _(e):
                p.emit_stream(e, "dve")

    return nc


def make_in_maps(inputs):
    f32 = lambda a: np.ascontiguousarray(np.asarray(a, dtype=np.float32))
    x = f32(inputs["L0_attn_x"])
    wq = f32(inputs["L0_attn_Wq"])
    bq = f32(inputs["L0_attn_bq"]).reshape(1, D)
    wk = f32(inputs["L0_attn_Wk"])
    bk = f32(inputs["L0_attn_bk"]).reshape(1, 256)
    wv = f32(inputs["L0_attn_Wv"])
    bv = f32(inputs["L0_attn_bv"]).reshape(256)
    wo = f32(inputs["L0_attn_Wo"])
    bo = f32(inputs["L0_attn_bo"]).reshape(1, D)
    gate_w = f32(inputs["L0_ffn_gate_w"])
    w_gate = f32(inputs["L0_ffn_w_gate"])
    w_up = f32(inputs["L0_ffn_w_up"])
    w_down = f32(inputs["L0_ffn_w_down"])
    sw_gate = f32(inputs["L0_ffn_sw_gate"])
    sw_up = f32(inputs["L0_ffn_sw_up"])
    sw_down = f32(inputs["L0_ffn_sw_down"])

    wv_aug = np.zeros((D, 260), np.float32)
    wv_aug[:, 0:128] = wv[:, 0:128]
    wv_aug[:, 130:258] = wv[:, 128:256]
    bv_aug = np.zeros((1, 260), np.float32)
    bv_aug[0, 0:128] = bv[0:128]
    bv_aug[0, 128] = 1.0
    bv_aug[0, 130:258] = bv[128:256]
    bv_aug[0, 258] = 1.0
    gwr = np.ascontiguousarray(gate_w.reshape(16, 128, 8).transpose(1, 0, 2))
    wg = np.ascontiguousarray(np.concatenate(
        [w_gate, sw_gate[:, 0:EH][None], sw_gate[:, EH:2 * EH][None]], 0))
    wu = np.ascontiguousarray(np.concatenate(
        [w_up, sw_up[:, 0:EH][None], sw_up[:, EH:2 * EH][None]], 0))
    wd = np.ascontiguousarray(np.concatenate(
        [w_down, sw_down[0:EH, :][None], sw_down[EH:2 * EH, :][None]], 0))
    ident = np.eye(128, dtype=np.float16)
    ident32 = np.eye(128, dtype=np.float32)

    f16 = np.float16
    wq16 = np.ascontiguousarray(
        wq.reshape(16, 128, 8, 256).transpose(2, 1, 0, 3).astype(f16))
    wk16 = np.ascontiguousarray(
        wk.reshape(16, 128, 256).transpose(1, 0, 2).astype(f16))
    wv16 = np.ascontiguousarray(
        wv_aug.reshape(16, 128, 260).transpose(1, 0, 2).astype(f16))
    wo16 = np.ascontiguousarray(
        wo.reshape(16, 128, 4, 512).transpose(2, 1, 0, 3).astype(f16))
    wgp = np.concatenate([wg, np.zeros((NE, D, 128), np.float32)], axis=2)
    wup = np.concatenate([wu, np.zeros((NE, D, 128), np.float32)], axis=2)
    wg16 = np.ascontiguousarray(
        wgp.reshape(NE, 16, 128, 6, 256).transpose(0, 3, 2, 1, 4).astype(f16))
    wu16 = np.ascontiguousarray(
        wup.reshape(NE, 16, 128, 6, 256).transpose(0, 3, 2, 1, 4).astype(f16))
    wd16 = np.ascontiguousarray(
        wd.reshape(NE, 11, 128, 4, 512).transpose(0, 3, 2, 1, 4).astype(f16))

    shared = dict(wq=wq16, bq=bq, wk=wk16, bk=bk, wv_aug=wv16, bv_aug=bv_aug,
                  wo=wo16, bo=bo, gwr=gwr, wg=wg16, wu=wu16, wd=wd16,
                  ident=ident, ident32=ident32)
    in_maps = []
    kq = np.arange(512)
    for c in range(8):
        s, r = c // 4, c % 4
        perm = np.arange(16)
        if r:
            for m in range(4):
                perm[4 * m], perm[4 * m + r] = perm[4 * m + r], perm[4 * m]
        xp = np.ascontiguousarray(
            x[s].reshape(16, 512, D)[perm].reshape(SEQ, D))
        mask = np.zeros((2048, 512), np.float32)
        mask[0:512] = (kq[:, None] <= kq[None, :])
        for j in range(1, 4):
            mask[512 * j:512 * j + 512] = 1.0 if j <= r else 0.0
        m16 = mask.astype(ml_dtypes.bfloat16)
        im = dict(shared)
        im["xseq"] = xp
        im["maskT"] = m16
        in_maps.append(im)
    return in_maps


def assemble_out(results):
    out = np.empty((2, SEQ, D), np.float32)
    for c in range(8):
        o = np.asarray(results[c]["out"], np.float32)
        s, r = c // 4, c % 4
        for row0, m in ((0, 3), (512, 0), (1024, 2), (1536, 1)):
            b = 4 * m + r
            out[s, 512 * b:512 * b + 512] = o[row0:row0 + 512]
    return out


def kernel(**inputs):
    in_maps = make_in_maps(inputs)
    nc = build_program()
    res = run_bass_kernel_spmd(nc, in_maps, core_ids=list(range(8)))
    return assemble_out(res.results)
